# version 30
# baseline (speedup 1.0000x reference)
import numpy as np
import ml_dtypes
from contextlib import ExitStack

import concourse.bass as bass
import concourse.mybir as mybir
from concourse.bass_utils import run_bass_kernel_spmd

F32 = mybir.dt.float32
BF16 = mybir.dt.bfloat16
AF = mybir.ActivationFunctionType
ALU = mybir.AluOpType
AX = mybir.AxisListType
BF = ml_dtypes.bfloat16

D = 2048
NT = 8192
NCORE = 8
TPC = NT // NCORE
NTILE_MY = TPC // 128
NTILE_ALL = NT // 128
NH = 16
NKV = 4
HD = 128
KV = 512
DFF = 5504
NDFF = DFF // 128
NCB = D // 128

C_EPS_LN = 0
C_EPS_RMS = 1
C_QBIAS = 2
C_VNEPS = 4
B_HEADS = NH
RMS_EPS = float(np.finfo(np.float32).eps)


class Sem:
    def __init__(self, nc, ctx, name):
        self.h = ctx.enter_context(nc.semaphore(name))
        self.n = 0

    def inc(self, instr, k=1):
        instr.then_inc(self.h, k)
        self.n += k
        return self.n

    def dma(self, instr):
        return self.inc(instr, 16)

    def wait(self, eng, n=None):
        eng.wait_ge(self.h, self.n if n is None else n)


class RecInstr:
    __slots__ = ("incs",)

    def __init__(self):
        self.incs = []

    def then_inc(self, sem, k):
        self.incs.append((sem, k))
        return self


class RecEngine:

    def __init__(self, q):
        self._q = q

    def __getattr__(self, op):
        q = self._q

        def call(*a, **kw):
            ri = RecInstr()
            q.append((op, a, kw, ri))
            return ri

        return call


REPLAY_CHUNK = 100000


def build(upto="full"):
    nc = bass.Bass("TRN2", target_bir_lowering=False)

    x_my = nc.dram_tensor("x_my", [NT, D], F32, kind="ExternalInput").ap()
    wqT = nc.dram_tensor("wqT", [D, D], F32, kind="ExternalInput").ap()
    wkT = nc.dram_tensor("wkT", [D, KV], F32, kind="ExternalInput").ap()
    wvT = nc.dram_tensor("wvT", [D, KV], F32, kind="ExternalInput").ap()
    woT = nc.dram_tensor("woT", [D, D], F32, kind="ExternalInput").ap()
    wg_r = nc.dram_tensor("wg_r", [NDFF, 128, D], F32, kind="ExternalInput").ap()
    wu_r = nc.dram_tensor("wu_r", [NDFF, 128, D], F32, kind="ExternalInput").ap()
    wd = nc.dram_tensor("wd", [DFF, D], F32, kind="ExternalInput").ap()
    rupa = nc.dram_tensor("rupa", [D, 72], F32, kind="ExternalInput").ap()
    rupr = nc.dram_tensor("rupr", [D, 8], F32, kind="ExternalInput").ap()
    rupb = nc.dram_tensor("rupb", [D, 72], F32, kind="ExternalInput").ap()
    vbg = nc.dram_tensor("vbg", [72, DFF], F32, kind="ExternalInput").ap()
    vbu = nc.dram_tensor("vbu", [72, DFF], F32, kind="ExternalInput").ap()
    udn = nc.dram_tensor("udn", [DFF, 72], F32, kind="ExternalInput").ap()
    vbd = nc.dram_tensor("vbd", [72, D], F32, kind="ExternalInput").ap()
    msel = nc.dram_tensor("msel", [9, 72], BF16, kind="ExternalInput").ap()
    ebias = nc.dram_tensor("ebias", [1, 8], F32, kind="ExternalInput").ap()
    eye = nc.dram_tensor("eye", [128, 128], BF16, kind="ExternalInput").ap()
    eyef = nc.dram_tensor("eyef", [128, 128], F32, kind="ExternalInput").ap()
    eyef8 = nc.dram_tensor("eyef8", [8, 8], F32, kind="ExternalInput").ap()
    onec = nc.dram_tensor("onec", [128, 1], BF16, kind="ExternalInput").ap()
    onecf = nc.dram_tensor("onecf", [128, 1], F32, kind="ExternalInput").ap()
    onerf = nc.dram_tensor("onerf", [1, 128], F32, kind="ExternalInput").ap()
    consts = nc.dram_tensor("consts", [128, 8], F32, kind="ExternalInput").ap()

    out = nc.dram_tensor("out", [TPC, D], F32, kind="ExternalOutput").ap()
    qtd = nc.dram_tensor("qtd", [128, NH * TPC], BF16, kind="Internal").ap()
    vnd = nc.dram_tensor("vnd", [128, NTILE_MY * KV], BF16, kind="Internal").ap()
    ydram = nc.dram_tensor("ydram", [128, NTILE_MY * D], BF16, kind="Internal").ap()

    dumps = []

    def dump(name, ap, shape, dtype):
        d = nc.dram_tensor(name, shape, F32, kind="ExternalOutput").ap()
        dumps.append((d, ap))

    ctx = ExitStack()
    eye_sb = ctx.enter_context(nc.sbuf_tensor("eye_sb", [128, 128], BF16))
    eyef_sb = ctx.enter_context(nc.sbuf_tensor("eyef_sb2", [128, 128], F32))
    eyef8_sb = ctx.enter_context(nc.sbuf_tensor("eyef8_sb", [8, 8], F32))
    onec_sb = ctx.enter_context(nc.sbuf_tensor("onec_sb", [128, 1], BF16))
    onecf_sb = ctx.enter_context(nc.sbuf_tensor("onecf_sb", [128, 1], F32))
    onerf_sb = ctx.enter_context(nc.sbuf_tensor("onerf_sb", [1, 128], F32))
    consts_sb = ctx.enter_context(nc.sbuf_tensor("consts_sb", [128, 8], F32))

    sq = Sem(nc, ctx, "sq")
    gq = Sem(nc, ctx, "gq")
    sqx = [Sem(nc, ctx, "sqx0"), Sem(nc, ctx, "sqx1"), Sem(nc, ctx, "sqx2")]
    qs = [Sem(nc, ctx, "qs0"), Sem(nc, ctx, "qs1")]
    swv = Sem(nc, ctx, "swv")
    swq = Sem(nc, ctx, "swq")
    swk = Sem(nc, ctx, "swk")
    swo = Sem(nc, ctx, "swo")
    srup = Sem(nc, ctx, "srup")
    syst = Sem(nc, ctx, "syst")
    syy = [Sem(nc, ctx, "syy0"), Sem(nc, ctx, "syy1")]
    sx2st = Sem(nc, ctx, "sx2st")
    swg = [Sem(nc, ctx, "swg0"), Sem(nc, ctx, "swg1")]
    swd = [Sem(nc, ctx, "swd0"), Sem(nc, ctx, "swd1")]
    sx2ld = Sem(nc, ctx, "sx2ld")
    sout = Sem(nc, ctx, "sout")
    svn = Sem(nc, ctx, "svn")
    gpc = Sem(nc, ctx, "gpc")
    pe = Sem(nc, ctx, "pe")
    act = Sem(nc, ctx, "act")
    dve = Sem(nc, ctx, "dve")

    queues = {name: [] for name in ("sync", "gpsimd", "vector", "scalar", "tensor")}
    recs = {name: RecEngine(q) for name, q in queues.items()}

    def on(name):
        return lambda f: f(recs[name])

    def replay():
        B = ctx.enter_context(nc.Block())
        for name in ("sync", "gpsimd", "vector", "scalar", "tensor"):
            q = queues[name]
            for c0 in range(0, len(q), REPLAY_CHUNK):
                chunk = q[c0:c0 + REPLAY_CHUNK]

                def emit(eng, chunk=chunk):
                    for op, a, kw, ri in chunk:
                        instr = getattr(eng, op)(*a, **kw)
                        for sem, k in ri.incs:
                            instr.then_inc(sem, k)

                getattr(B, name)(emit)

    @on("gpsimd")
    def _(g):
        gq.dma(g.dma_start(out=eye_sb[:, :], in_=eye))
        gq.dma(g.dma_start(out=eyef_sb[:, :], in_=eyef))
        gq.dma(g.dma_start(out=eyef8_sb[:, :], in_=eyef8))
        gq.dma(g.dma_start(out=onec_sb[:, :], in_=onec))
        gq.dma(g.dma_start(out=onecf_sb[:, :], in_=onecf))
        gq.dma(g.dma_start(out=onerf_sb[:, :], in_=onerf))
        gq.dma(g.dma_start(out=consts_sb[:, :], in_=consts))
    n_consts = gq.n

    @on("vector")
    def _(v):
        gq.wait(v, n_consts)

    @on("scalar")
    def _(a):
        gq.wait(a, n_consts)

    @on("tensor")
    def _(te):
        gq.wait(te, n_consts)

    aC = ExitStack()
    wv_sb = aC.enter_context(nc.sbuf_tensor("wv_sb", [128, NCB * KV], BF16))
    wk_sb = aC.enter_context(nc.sbuf_tensor("wk_sb", [128, NCB * KV], BF16))
    xa = [aC.enter_context(nc.sbuf_tensor(f"xa{i}", [128, D], F32)) for i in range(3)]
    xn = [aC.enter_context(nc.sbuf_tensor(f"xn{i}", [128, D], BF16)) for i in range(2)]
    xnT = [aC.enter_context(nc.sbuf_tensor(f"xnT{i}", [128, D], BF16)) for i in range(2)]
    st8 = [aC.enter_context(nc.sbuf_tensor(f"st8_{i}", [128, 8], F32)) for i in range(2)]
    rv4 = [aC.enter_context(nc.sbuf_tensor(f"rv4_{i}", [128, 4], F32)) for i in range(2)]
    vn1 = [aC.enter_context(nc.sbuf_tensor(f"vn1_{i}", [128, KV], BF16)) for i in range(2)]
    rk4 = [aC.enter_context(nc.sbuf_tensor(f"rk4_{i}", [128, 4], F32)) for i in range(2)]
    khat = [aC.enter_context(nc.sbuf_tensor(f"khat{i}", [128, KV], BF16)) for i in range(2)]
    tp_ps = [aC.enter_context(nc.psum_tensor(f"tp_ps{i}", [128, 1024], BF16)) for i in range(2)]

    a0w = ExitStack()
    wq_sb = a0w.enter_context(nc.sbuf_tensor("wq_sb", [128, NCB * D], BF16))
    qtstage = a0w.enter_context(nc.sbuf_tensor("qtstage", [128, NH * TPC], BF16))
    qhat = a0w.enter_context(nc.sbuf_tensor("qhat", [128, 2 * D], BF16))

    @on("gpsimd")
    def _(g):
        swv.dma(g.dma_start(out=wv_sb[:, :].rearrange("p (c o) -> p c o", c=NCB),
                            in_=wvT.rearrange("(c p) o -> p c o", p=128)))
        swq.dma(g.dma_start(out=wq_sb[:, :].rearrange("p (c o) -> p c o", c=NCB),
                            in_=wqT.rearrange("(c p) o -> p c o", p=128)))
        swk.dma(g.dma_start(out=wk_sb[:, :].rearrange("p (c o) -> p c o", c=NCB),
                            in_=wkT.rearrange("(c p) o -> p c o", p=128)))

    @on("sync")
    def _(s):
        for t in range(3):
            sqx[t].dma(s.dma_start(out=xa[t][:, :], in_=x_my[t * 128:(t + 1) * 128, :]))

    NQS = NTILE_MY * 2
    qps2 = [a0w.enter_context(nc.psum_tensor(f"qps2_{i}", [128, 1024], F32)) for i in range(2)]
    rq8 = [a0w.enter_context(nc.sbuf_tensor(f"rq8_{i}", [128, 8], F32)) for i in range(2)]

    qp = dict(sq={}, rstd={}, var={}, xn={}, tpe={}, tr={}, mm={}, qsq={}, qred={},
              qrms={}, qsc={}, qtr={}, qev={})

    def qslot(s):
        t, hf = divmod(s, 2)
        return hf * 2 + (t % 2)

    for s in range(-2, NQS + 6):
        if 0 <= s < NQS:
            t, hf = divmod(s, 2)

            @on("tensor")
            def _(te, s=s, t=t, hf=hf):
                dve.wait(te, qp["xn"][t])
                dve.wait(te, qp["tpe"].get(s - 1, 0))
                dve.wait(te, qp["qev"].get(s - 6, 0))
                for j in range(8):
                    c = hf * 8 + j
                    pe.inc(te.transpose(tp_ps[s % 2][:, j * 128:(j + 1) * 128],
                                        xn[t % 2][:, c * 128:(c + 1) * 128], eye_sb[:, :]))
            qp["tr"][s] = pe.n
        if 0 <= s - 5 < NQS:
            s3 = s - 5
            q3 = qslot(s3)

            @on("tensor")
            def _(te, s3=s3, q3=q3):
                dve.wait(te, qp["qsc"][s3])
                dve.wait(te, qp["qev"].get(s3 - 1, 0))
                dve.wait(te, qp["tpe"].get(s3 + 4, 0))
                for j in range(8):
                    pe.inc(te.transpose(tp_ps[s3 % 2][:, j * 128:(j + 1) * 128],
                                        qhat[:, q3 * 1024 + j * 128: q3 * 1024 + (j + 1) * 128],
                                        eye_sb[:, :]))
            qp["qtr"][s3] = pe.n
        if 0 <= s - 2 < NQS:
            s1 = s - 2
            t1, hf1 = divmod(s1, 2)

            @on("tensor")
            def _(te, s1=s1, t1=t1, hf1=hf1):
                if s1 == 0:
                    swq.wait(te)
                dve.wait(te, qp["tpe"][2 * t1 + 1])
                dve.wait(te, qp["qsc"].get(s1 - 2, 0))
                for cb in range(NCB):
                    lhs = xnT[t1 % 2][:, cb * 128:(cb + 1) * 128]
                    for oc in range(2):
                        o0 = hf1 * 1024 + oc * 512
                        pe.inc(te.matmul(qps2[s1 % 2][:, oc * 512:(oc + 1) * 512], lhs,
                                         wq_sb[:, cb * D + o0: cb * D + o0 + 512],
                                         start=(cb == 0), stop=(cb == NCB - 1)))
            qp["mm"][s1] = pe.n

        if 0 <= s - 3 < NQS:
            s2q = s - 3

            @on("scalar")
            def _(a, s2q=s2q):
                act.wait(a)
                dve.wait(a, qp["qred"][s2q])
                act.inc(a.activation(rq8[s2q % 2][:, :], rq8[s2q % 2][:, :], AF.Ln, scale=1.0 / HD,
                                     bias=consts_sb[:, C_EPS_RMS:C_EPS_RMS + 1]))
                act.wait(a)
                act.inc(a.activation(rq8[s2q % 2][:, :], rq8[s2q % 2][:, :], AF.Exp, scale=-0.5,
                                     bias=consts_sb[:, C_QBIAS:C_QBIAS + 1]))
            qp["qrms"][s2q] = act.n
        if 0 <= s + 2 < NQS and (s + 2) % 2 == 0:
            tq = (s + 2) // 2

            @on("scalar")
            def _(a, tq=tq):
                act.wait(a)
                sqx[tq % 3].wait(a, 16 * (tq // 3 + 1))
                pe.wait(a, qp["tr"].get(2 * (tq - 2) + 1, 0))
                act.inc(a.activation(xn[tq % 2][:, :], xa[tq % 3][:, :], AF.Square, accum_out=st8[tq % 2][:, 1:2]))
            qp["sq"][tq] = act.n
        if 0 <= s + 1 < NQS and (s + 1) % 2 == 0:
            tq = (s + 1) // 2

            @on("scalar")
            def _(a, tq=tq):
                act.wait(a)
                dve.wait(a, qp["var"][tq])
                act.inc(a.activation(st8[tq % 2][:, 4:5], st8[tq % 2][:, 2:3], AF.Ln, scale=1.0 / D,
                                     bias=consts_sb[:, C_EPS_LN:C_EPS_LN + 1]))
                act.wait(a)
                act.inc(a.activation(st8[tq % 2][:, 3:4], st8[tq % 2][:, 4:5], AF.Exp, scale=-0.5))
            qp["rstd"][tq] = act.n
        if 0 <= s - 2 < NQS:
            s1 = s - 2
            q1 = qslot(s1)

            @on("scalar")
            def _(a, s1=s1, q1=q1):
                act.wait(a)
                pe.wait(a, qp["mm"][s1])
                pe.wait(a, qp["qtr"].get(s1 - 4, 0))
                act.inc(a.activation(qhat[:, q1 * 1024:(q1 + 1) * 1024], qps2[s1 % 2][:, :], AF.Square))
            qp["qsq"][s1] = act.n

        if 0 <= s + 2 < NQS and (s + 2) % 2 == 0:
            tq = (s + 2) // 2

            @on("vector")
            def _(v, tq=tq):
                dve.wait(v)
                sqx[tq % 3].wait(v, 16 * (tq // 3 + 1))
                dve.inc(v.tensor_reduce(st8[tq % 2][:, 0:1], xa[tq % 3][:, :], AX.X, ALU.add))
                act.wait(v, qp["sq"][tq])
                dve.wait(v)
                dve.inc(v.tensor_tensor(st8[tq % 2][:, 2:3], st8[tq % 2][:, 0:1], st8[tq % 2][:, 0:1], ALU.mult))
                dve.wait(v)
                dve.inc(v.scalar_tensor_tensor(st8[tq % 2][:, 2:3], st8[tq % 2][:, 2:3], -1.0 / D,
                                               st8[tq % 2][:, 1:2], ALU.mult, ALU.add))
            qp["var"][tq] = dve.n
        if 0 <= s + 1 < NQS and (s + 1) % 2 == 0:
            tq = (s + 1) // 2

            @on("vector")
            def _(v, tq=tq):
                dve.wait(v)
                act.wait(v, qp["rstd"][tq])
                pe.wait(v, qp["tr"].get(2 * (tq - 1) + 1, 0))
                dve.inc(v.tensor_tensor(st8[tq % 2][:, 5:6], st8[tq % 2][:, 0:1], st8[tq % 2][:, 3:4], ALU.mult))
                dve.wait(v)
                dve.inc(v.tensor_scalar(st8[tq % 2][:, 5:6], st8[tq % 2][:, 5:6], -1.0 / D, None, ALU.mult))
                dve.wait(v)
                dve.inc(v.tensor_scalar(xn[tq % 2][:, :], xa[tq % 3][:, :], st8[tq % 2][:, 3:4],
                                        st8[tq % 2][:, 5:6], ALU.mult, ALU.add))
            qp["xn"][tq] = dve.n
        if 0 <= s < NQS:
            t, hf = divmod(s, 2)

            @on("vector")
            def _(v, s=s, t=t, hf=hf):
                dve.wait(v)
                pe.wait(v, qp["tr"][s])
                pe.wait(v, qp["mm"].get(2 * (t - 1) + 1, 0))
                dve.inc(v.tensor_copy(xnT[t % 2][:, hf * 1024:(hf + 1) * 1024], tp_ps[s % 2][:, :]))
            qp["tpe"][s] = dve.n
        if 0 <= s - 3 < NQS:
            s2q = s - 3
            q2 = qslot(s2q)

            @on("vector")
            def _(v, s2q=s2q, q2=q2):
                dve.wait(v)
                act.wait(v, qp["qrms"][s2q])
                for j in range(8):
                    dve.inc(v.tensor_scalar(qhat[:, q2 * 1024 + j * 128: q2 * 1024 + (j + 1) * 128],
                                            qps2[s2q % 2][:, j * 128:(j + 1) * 128],
                                            rq8[s2q % 2][:, j:j + 1], None, ALU.mult))
            qp["qsc"][s2q] = dve.n
        if 0 <= s - 5 < NQS:
            s3 = s - 5
            t3, hf3 = divmod(s3, 2)

            @on("vector")
            def _(v, s3=s3, t3=t3, hf3=hf3):
                dve.wait(v)
                pe.wait(v, qp["qtr"][s3])
                dve.inc(v.tensor_copy(
                    qtstage[:, :].rearrange("p (h q) -> p h q", h=NH)[:, hf3 * 8:(hf3 + 1) * 8,
                                                                     t3 * 128:(t3 + 1) * 128],
                    tp_ps[s3 % 2][:, :].rearrange("p (j q) -> p j q", j=8)))
            qp["qev"][s3] = dve.n
        if 0 <= s - 2 < NQS:
            s1 = s - 2
            q1 = qslot(s1)

            @on("vector")
            def _(v, s1=s1, q1=q1):
                dve.wait(v)
                act.wait(v, qp["qsq"][s1])
                dve.inc(v.tensor_reduce(rq8[s1 % 2][:, :],
                                        qhat[:, q1 * 1024:(q1 + 1) * 1024].rearrange("p (h d) -> p h d", h=8),
                                        AX.X, ALU.add))
            qp["qred"][s1] = dve.n
        if 0 <= s - 1 < NQS and (s - 1) % 2 == 0:
            tq = (s - 1) // 2 + 3
            if tq < NTILE_MY:
                @on("sync")
                def _(s_, tq=tq):
                    dve.wait(s_, qp["xn"].get(tq - 3, 0))
                    sqx[tq % 3].dma(s_.dma_start(out=xa[tq % 3][:, :], in_=x_my[tq * 128:(tq + 1) * 128, :]))

    qt_evac = dve.n
    @on("sync")
    def _(s):
        dve.wait(s, qt_evac)
        sq.dma(s.dma_start(out=qtd[:, :], in_=qtstage[:, :]))
    qtd_stored = sq.n

    a0_pe = pe.n
    a0_act = act.n
    a0_dve = dve.n
    a0_hand = dve.n
    a0_sqx = [sqx[i].n for i in range(3)]
    a0w.close()
    kvY = ExitStack()
    KT_sb = kvY.enter_context(nc.sbuf_tensor("KT_sb", [128, NKV * NT], BF16, side="right"))
    V_sb = kvY.enter_context(nc.sbuf_tensor("V_sb", [128, NTILE_ALL * KV], BF16, side="right"))

    a1P = ExitStack()
    kps2 = [a1P.enter_context(nc.psum_tensor(f"kps2_{i}", [128, KV], F32)) for i in range(2)]
    vps2 = [a1P.enter_context(nc.psum_tensor(f"vps2_{i}", [128, KV], F32)) for i in range(2)]
    ktp_ps = a1P.enter_context(nc.psum_tensor("ktp_ps", [128, KV], BF16))
    ksq_ps = a1P.enter_context(nc.psum_tensor("ksq_ps", [128, KV], F32))

    lo, hi = 0, NTILE_ALL
    sqd, rstdd, var_d, xnd, tped, trd, mmd = {}, {}, {}, {}, {}, {}, {}
    ksqd, kredd, krmsd, khatd, vcopd, ktrd, kted = {}, {}, {}, {}, {}, {}, {}
    vsqd, vredd, vrmsd, vnd_d = {}, {}, {}, {}

    for i in range(lo - 3, hi + 3):
        t = i + 3
        if lo + 3 <= t < hi:
            @on("sync")
            def _(s, t=t):
                dve.wait(s, xnd.get(t - 3, a0_dve))
                sqx[t % 3].dma(s.dma_start(out=xa[t % 3][:, :], in_=x_my[t * 128:(t + 1) * 128, :]))
        elif lo <= t < lo + 3:
            @on("sync")
            def _(s, t=t):
                dve.wait(s, a0_dve)
                sqx[t % 3].dma(s.dma_start(out=xa[t % 3][:, :], in_=x_my[t * 128:(t + 1) * 128, :]))

        t = i
        if lo <= t < hi:
            @on("tensor")
            def _(te, t=t):
                if t == lo:
                    swk.wait(te)
                    swv.wait(te)
                dve.wait(te, xnd[t])
                dve.wait(te, tped.get(t - 1, a0_dve))
                for r in range(2):
                    for j in range(8):
                        c = r * 8 + j
                        pe.inc(te.transpose(tp_ps[r][:, j * 128:(j + 1) * 128],
                                            xn[t % 2][:, c * 128:(c + 1) * 128], eye_sb[:, :]))
            trd[t] = pe.n
        t = i - 3
        if lo <= t < hi:
            @on("tensor")
            def _(te, t=t):
                dve.wait(te, khatd[t])
                dve.wait(te, kted.get(t - 1, a0_dve))
                if t == lo:
                    dve.wait(te, a0_hand)
                for j in range(NKV):
                    pe.inc(te.transpose(ktp_ps[:, j * 128:(j + 1) * 128],
                                        khat[t % 2][:, j * 128:(j + 1) * 128], eye_sb[:, :]))
            ktrd[t] = pe.n
        t = i - 1
        if lo <= t < hi:
            @on("tensor")
            def _(te, t=t):
                dve.wait(te, tped[t])
                act.wait(te, ksqd.get(t - 2, a0_act))
                dve.wait(te, vcopd.get(t - 2, a0_dve))
                if t < lo + 2:
                    dve.wait(te, a0_hand)
                for cb in range(NCB):
                    lhs = xnT[t % 2][:, cb * 128:(cb + 1) * 128]
                    pe.inc(te.matmul(kps2[t % 2][:, :], lhs, wk_sb[:, cb * KV:(cb + 1) * KV],
                                     start=(cb == 0), stop=(cb == NCB - 1)))
                    pe.inc(te.matmul(vps2[t % 2][:, :], lhs, wv_sb[:, cb * KV:(cb + 1) * KV],
                                     start=(cb == 0), stop=(cb == NCB - 1)))
            mmd[t] = pe.n

        t = i - 2
        if lo <= t < hi:
            @on("scalar")
            def _(a, t=t):
                act.wait(a)
                dve.wait(a, kredd[t])
                act.inc(a.activation(rk4[t % 2][:, :], rk4[t % 2][:, :], AF.Ln, scale=1.0 / HD,
                                     bias=consts_sb[:, C_EPS_RMS:C_EPS_RMS + 1]))
                act.wait(a)
                act.inc(a.activation(rk4[t % 2][:, :], rk4[t % 2][:, :], AF.Exp, scale=-0.5))
            krmsd[t] = act.n
        t = i - 2
        if lo <= t < min(NTILE_MY, hi):
            @on("scalar")
            def _(a, t=t):
                act.wait(a)
                dve.wait(a, vredd[t])
                act.inc(a.activation(rv4[t % 2][:, :], rv4[t % 2][:, :], AF.Ln,
                                     bias=consts_sb[:, C_VNEPS:C_VNEPS + 1]))
                act.wait(a)
                act.inc(a.activation(rv4[t % 2][:, :], rv4[t % 2][:, :], AF.Exp, scale=-0.5))
            vrmsd[t] = act.n
        t = i + 2
        if lo <= t < hi:
            @on("scalar")
            def _(a, t=t):
                act.wait(a)
                sqx[t % 3].wait(a, a0_sqx[t % 3] + 16 * (t // 3 + 1))
                pe.wait(a, trd.get(t - 2, a0_pe))
                act.inc(a.activation(xn[t % 2][:, :], xa[t % 3][:, :], AF.Square, accum_out=st8[t % 2][:, 1:2]))
            sqd[t] = act.n
        t = i + 1
        if lo <= t < hi:
            @on("scalar")
            def _(a, t=t):
                act.wait(a)
                dve.wait(a, var_d[t])
                act.inc(a.activation(st8[t % 2][:, 4:5], st8[t % 2][:, 2:3], AF.Ln, scale=1.0 / D,
                                     bias=consts_sb[:, C_EPS_LN:C_EPS_LN + 1]))
                act.wait(a)
                act.inc(a.activation(st8[t % 2][:, 3:4], st8[t % 2][:, 4:5], AF.Exp, scale=-0.5))
            rstdd[t] = act.n
        t = i - 1
        if lo <= t < hi:
            @on("scalar")
            def _(a, t=t):
                act.wait(a)
                pe.wait(a, mmd[t])
                dve.wait(a, kredd.get(t - 1, a0_dve))
                act.inc(a.activation(ksq_ps[:, :], kps2[t % 2][:, :], AF.Square))
                if t < NTILE_MY:
                    svn.wait(a, vnd_d.get(t - 2, 0))
                    act.inc(a.activation(vn1[t % 2][:, :], vps2[t % 2][:, :], AF.Square))
            ksqd[t] = act.n

        t = i + 2
        if lo <= t < hi:
            @on("vector")
            def _(v, t=t):
                dve.wait(v)
                sqx[t % 3].wait(v, a0_sqx[t % 3] + 16 * (t // 3 + 1))
                dve.inc(v.tensor_reduce(st8[t % 2][:, 0:1], xa[t % 3][:, :], AX.X, ALU.add))
                act.wait(v, sqd[t])
                dve.wait(v)
                dve.inc(v.tensor_tensor(st8[t % 2][:, 2:3], st8[t % 2][:, 0:1], st8[t % 2][:, 0:1], ALU.mult))
                dve.wait(v)
                dve.inc(v.scalar_tensor_tensor(st8[t % 2][:, 2:3], st8[t % 2][:, 2:3], -1.0 / D,
                                               st8[t % 2][:, 1:2], ALU.mult, ALU.add))
            var_d[t] = dve.n
        t = i + 1
        if lo <= t < hi:
            @on("vector")
            def _(v, t=t):
                dve.wait(v)
                act.wait(v, rstdd[t])
                pe.wait(v, trd.get(t - 2, a0_pe))
                dve.inc(v.tensor_tensor(st8[t % 2][:, 5:6], st8[t % 2][:, 0:1], st8[t % 2][:, 3:4], ALU.mult))
                dve.wait(v)
                dve.inc(v.tensor_scalar(st8[t % 2][:, 5:6], st8[t % 2][:, 5:6], -1.0 / D, None, ALU.mult))
                dve.wait(v)
                dve.inc(v.tensor_scalar(xn[t % 2][:, :], xa[t % 3][:, :], st8[t % 2][:, 3:4],
                                        st8[t % 2][:, 5:6], ALU.mult, ALU.add))
            xnd[t] = dve.n
        t = i
        if lo <= t < hi:
            @on("vector")
            def _(v, t=t):
                dve.wait(v)
                pe.wait(v, trd[t])
                pe.wait(v, mmd.get(t - 2, a0_pe))
                dve.inc(v.tensor_copy(xnT[t % 2][:, 0:1024], tp_ps[0][:, :]))
                dve.wait(v)
                dve.inc(v.tensor_copy(xnT[t % 2][:, 1024:2048], tp_ps[1][:, :]))
            tped[t] = dve.n
        t = i - 2
        if lo <= t < hi:
            @on("vector")
            def _(v, t=t):
                dve.wait(v)
                act.wait(v, krmsd[t])
                pe.wait(v, ktrd.get(t - 2, a0_pe))
                for j in range(NKV):
                    dve.inc(v.tensor_scalar(khat[t % 2][:, j * 128:(j + 1) * 128],
                                            kps2[t % 2][:, j * 128:(j + 1) * 128],
                                            rk4[t % 2][:, j:j + 1], None, ALU.mult))
                if t < lo + 2:
                    sq.wait(v, qtd_stored)
                    pe.wait(v, a0_pe)
                    act.wait(v, a0_act)
                dve.wait(v)
                dve.inc(v.tensor_copy(V_sb[:, t * KV:(t + 1) * KV], vps2[t % 2][:, :]))
                if t < NTILE_MY:
                    act.wait(v, vrmsd[t])
                    for j in range(NKV):
                        dve.inc(v.tensor_scalar(vn1[t % 2][:, j * 128:(j + 1) * 128],
                                                vps2[t % 2][:, j * 128:(j + 1) * 128],
                                                rv4[t % 2][:, j:j + 1], None, ALU.mult))
            khatd[t] = dve.n
            vcopd[t] = dve.n
            if t < NTILE_MY:
                @on("sync")
                def _(s, t=t):
                    dve.wait(s, vcopd[t])
                    svn.dma(s.dma_start(out=vnd[:, t * KV:(t + 1) * KV], in_=vn1[t % 2][:, :]))
                vnd_d[t] = svn.n
        t = i - 3
        if lo <= t < hi:
            @on("vector")
            def _(v, t=t):
                dve.wait(v)
                pe.wait(v, ktrd[t])
                if t < lo + 2:
                    sq.wait(v, qtd_stored)
                dve.inc(v.tensor_copy(
                    KT_sb[:, :].rearrange("p (h q) -> p h q", h=NKV)[:, :, t * 128:(t + 1) * 128],
                    ktp_ps[:, :].rearrange("p (j q) -> p j q", j=NKV)))
            kted[t] = dve.n
        t = i - 1
        if lo <= t < hi:
            @on("vector")
            def _(v, t=t):
                dve.wait(v)
                act.wait(v, ksqd[t])
                dve.inc(v.tensor_reduce(rk4[t % 2][:, :],
                                        ksq_ps[:, :].rearrange("p (h d) -> p h d", h=4),
                                        AX.X, ALU.add))
                if t < NTILE_MY:
                    dve.wait(v)
                    dve.inc(v.tensor_reduce(rv4[t % 2][:, :],
                                            vn1[t % 2][:, :].rearrange("p (h d) -> p h d", h=4),
                                            AX.X, ALU.add))
            kredd[t] = dve.n
            vredd[t] = dve.n

    if upto == "A":
        dump("d_kt", KT_sb[:, :], [128, NKV * NT], BF16)
        dump("d_v", V_sb[:, :], [128, NTILE_ALL * KV], BF16)
        dump("d_vn", vnd, [128, NTILE_MY * KV], BF16)
        dump("d_qt", qtd, [128, NH * TPC], BF16)
        finish(recs, dumps, [sq, gq, pe, act, dve, svn])
        replay()
        return nc

    a1P.close()
    aC.close()

    Y_sb = kvY.enter_context(nc.sbuf_tensor("Y_sb", [128, NTILE_MY * D], BF16, side="right"))
    bC = ExitStack()
    vn_sb = bC.enter_context(nc.sbuf_tensor("vn_sb", [128, NTILE_MY * KV], BF16))
    padd = [bC.enter_context(nc.sbuf_tensor(f"padd{i}", [128, 512], BF16)) for i in range(2)]
    pacc = [bC.enter_context(nc.sbuf_tensor(f"pacc{i}", [128, 512], F32)) for i in range(4)]
    rdnT_sb = bC.enter_context(nc.sbuf_tensor("rdnT_sb", [128, 4], F32))
    qth = [bC.enter_context(nc.sbuf_tensor(f"qth{i}", [128, TPC], BF16)) for i in range(2)]
    P_sb = [bC.enter_context(nc.sbuf_tensor(f"P_sb{i}", [128, 1024], BF16)) for i in range(4)]
    yTs_sb = bC.enter_context(nc.sbuf_tensor("yTs_sb", [128, 512], BF16))
    cscr = bC.enter_context(nc.sbuf_tensor("cscr", [128, 128], BF16))
    dot_sb = bC.enter_context(nc.sbuf_tensor("dot_sb", [128, 4], F32))

    S_ps = [bC.enter_context(nc.psum_tensor(f"S_ps{i}", [128, 1024], F32)) for i in range(2)]
    yT_ps = [bC.enter_context(nc.psum_tensor(f"yT_ps{i}", [128, 512], F32)) for i in range(2)]
    epi_ps = bC.enter_context(nc.psum_tensor("epi_ps", [128, 8], F32))
    tt_ps = bC.enter_context(nc.psum_tensor("tt_ps", [128, 512], BF16))

    NG = NTILE_ALL // 2

    qth_free = [0, 0]
    qth_loaded = [0, 0]
    dn_mm = 0
    pending_epi = [None, None]
    last_paddd = [{}]
    epi_done = dve.n
    corr_done = dve.n
    b_psum_init = dve.n

    @on("sync")
    def _(s):
        sq.wait(s, qtd_stored)
        svn.wait(s)
        dve.wait(s)
        pe.wait(s)
        act.wait(s)
        qs[0].dma(s.dma_start(out=qth[0][:, :], in_=qtd[:, 0:TPC]))
        sq.dma(s.dma_start(out=vn_sb[:, :], in_=vnd))
    qth_loaded[0] = qs[0].n
    vn_loaded = sq.n

    for h in range(B_HEADS):
        par = h % 2
        kvh = h // NKV
        if h + 1 < NH:
            @on("sync")
            def _(s):
                if h >= 1:
                    pe.wait(s, qth_free[(h + 1) % 2])
                qs[(h + 1) % 2].dma(s.dma_start(out=qth[(h + 1) % 2][:, :], in_=qtd[:, (h + 1) * TPC:(h + 2) * TPC]))
            qth_loaded[(h + 1) % 2] = qs[(h + 1) % 2].n

        for qb in range(2):
            qbp = (h * 2 + qb) % 2
            rhs_q = qth[par][:, qb * 512:(qb + 1) * 512]
            e_done = [0] * NG
            pv_done = [0] * NG
            s_done = [0] * NG
            prev_paddd = last_paddd[0]
            paddd = {}
            paccd = {}
            dnd = {}
            for gi in range(NG + 2):
                if gi == 8 and pending_epi[0] is not None:
                    pending_epi[0]()
                    pending_epi[0] = None
                if gi == 14 and pending_epi[1] is not None:
                    pending_epi[1]()
                    pending_epi[1] = None
                if 1 <= gi <= NG:
                    g = gi - 1

                    @on("vector")
                    def _(v, g=g):
                        dve.wait(v)
                        act.wait(v, e_done[g])
                        gpc.wait(v, paccd.get(g - 2, 0))
                        dve.inc(v.tensor_tensor(padd[g % 2][:, :], P_sb[g % 4][:, 0:512],
                                                P_sb[g % 4][:, 512:1024], ALU.add))
                    paddd[g] = dve.n

                    @on("gpsimd")
                    def _(gg, g=g):
                        gpc.wait(gg)
                        dve.wait(gg, paddd[g])
                        pa = pacc[qbp * 2 + g % 2]
                        if g < 2:
                            pe.wait(gg, dn_mm)
                            gpc.inc(gg.tensor_copy(pa[:, :], padd[g % 2][:, :]))
                        else:
                            gpc.inc(gg.tensor_tensor(pa[:, :], pa[:, :], padd[g % 2][:, :], ALU.add))
                    paccd[g] = gpc.n
                if gi < NG:
                    g = gi
                    gp = g % 2

                    @on("tensor")
                    def _(te):
                        if qb == 0 and g == 0:
                            qs[par].wait(te, qth_loaded[par])
                        if h == 0 and qb == 0 and g <= 1:
                            dve.wait(te, b_psum_init)
                        if g >= 2:
                            act.wait(te, e_done[g - 2])
                        for j in range(2):
                            kt = 2 * g + j
                            pe.inc(te.matmul(
                                S_ps[gp][:, j * 512:(j + 1) * 512],
                                KT_sb[:, kvh * NT + kt * 128: kvh * NT + (kt + 1) * 128],
                                rhs_q, start=True, stop=True))
                    s_done[g] = pe.n

                    @on("scalar")
                    def _(a):
                        act.wait(a)
                        pe.wait(a, s_done[g])
                        if g >= 4:
                            pe.wait(a, pv_done[g - 4])
                            dve.wait(a, paddd[g - 4])
                        else:
                            dve.wait(a, prev_paddd.get(28 + g, 0))
                        act.inc(a.activation(P_sb[g % 4][:, :], S_ps[g % 2][:, :], AF.Exp))
                    e_done[g] = act.n

                if gi >= 2:
                    g = gi - 2

                    @on("tensor")
                    def _(te):
                        act.wait(te, e_done[g])
                        if g == 0:
                            dve.wait(te, epi_done)
                        for j in range(2):
                            kt = 2 * g + j
                            pe.inc(te.matmul(
                                yT_ps[qbp][:, :],
                                V_sb[:, kt * KV + kvh * 128: kt * KV + (kvh + 1) * 128],
                                P_sb[g % 4][:, j * 512:(j + 1) * 512],
                                start=(g == 0 and j == 0), stop=(g == NG - 1 and j == 1)))
                    pv_done[g] = pe.n

            def make_epi(h=h, qb=qb, par=par, kvh=kvh, qbp=qbp, e_done=e_done, pv_done=pv_done,
                         s_done=s_done, paddd=paddd, paccd=paccd, dnd=dnd):
              nonlocal dn_mm, epi_done, corr_done
              @on("tensor")
              def _(te):
                  gpc.wait(te, paccd[NG - 1])
                  dve.wait(te, epi_done)
                  for j in range(4):
                      pe.inc(te.matmul(epi_ps[:, j:j + 1], pacc[qbp * 2][:, j * 128:(j + 1) * 128],
                                       onecf_sb[:, 0:1], start=True, stop=False))
                      pe.inc(te.matmul(epi_ps[:, j:j + 1], pacc[qbp * 2 + 1][:, j * 128:(j + 1) * 128],
                                       onecf_sb[:, 0:1], start=False, stop=True))
              dn_mm = pe.n

              @on("vector")
              def _(v):
                  dve.wait(v)
                  pe.wait(v, dn_mm)
                  pe.wait(v, pv_done[NG - 1])
                  dve.inc(v.reciprocal(rdnT_sb[:, :], epi_ps[:, 0:4]))
                  dve.wait(v)
                  dve.inc(v.tensor_scalar(rdnT_sb[:, :], rdnT_sb[:, :], -1.0, None, ALU.mult))
                  dve.wait(v)
                  dve.inc(v.tensor_copy(yTs_sb[:, :], yT_ps[qbp][:, :]))
              rdn_done = dve.n
              ydiv_done = dve.n
              epi_done = dve.n

              def make_epi_b(qbp=qbp, h=h, qb=qb, kvh=kvh):
                nonlocal corr_done
                @on("tensor")
                def _(te):
                    dve.wait(te, ydiv_done)
                    dve.wait(te, corr_done)
                    for j in range(4):
                        pe.inc(te.transpose(tt_ps[:, j * 128:(j + 1) * 128],
                                                   yTs_sb[:, j * 128:(j + 1) * 128], eye_sb[:, :]))
                ytr_done = pe.n

                @on("vector")
                def _(v):
                    dve.wait(v)
                    pe.wait(v, ytr_done)
                    if h == 0 and qb == 0:
                        sq.wait(v, vn_loaded)
                    for j in range(4):
                        qt = qb * 4 + j
                        yv = tt_ps[:, j * 128:(j + 1) * 128]
                        vnv = vn_sb[:, qt * KV + kvh * 128: qt * KV + (kvh + 1) * 128]
                        dve.wait(v)
                        dve.inc(v.tensor_tensor(cscr[:, :], yv, vnv, ALU.mult))
                        dve.wait(v)
                        dve.inc(v.tensor_reduce(dot_sb[:, j:j + 1], cscr[:, :], AX.X, ALU.add))
                    for j in range(4):
                        qt = qb * 4 + j
                        yv = tt_ps[:, j * 128:(j + 1) * 128]
                        vnv = vn_sb[:, qt * KV + kvh * 128: qt * KV + (kvh + 1) * 128]
                        dve.wait(v)
                        dve.inc(v.scalar_tensor_tensor(cscr[:, :], vnv, dot_sb[:, j:j + 1], yv,
                                                       ALU.mult, ALU.subtract))
                        dve.wait(v)
                        dve.inc(v.tensor_scalar(
                            Y_sb[:, qt * D + h * 128: qt * D + (h + 1) * 128],
                            cscr[:, :], rdnT_sb[:, j:j + 1], None, ALU.mult))
                corr_done = dve.n
              pending_epi[1] = make_epi_b

            pending_epi[0] = make_epi
            last_paddd[0] = paddd
        qth_free[par] = pe.n

    pending_epi[0]()
    pending_epi[0] = None
    pending_epi[1]()
    pending_epi[1] = None

    if upto == "B":
        dump("d_y", Y_sb[:, :], [128, NTILE_MY * D], BF16)
        dump("d_yts", yTs_sb[:, :], [128, 512], BF16)
        dump("d_p", P_sb[1][:, :], [128, 1024], BF16)
        dump("d_qth", qth[0][:, :], [128, TPC], BF16)
        dump("d_kt", KT_sb[:, :], [128, NKV * NT], BF16)
        finish(recs, dumps, [sq, gq, pe, act, dve])
        replay()
        return nc

    b_end_dve = dve.n
    @on("sync")
    def _(s):
        dve.wait(s, b_end_dve)
        syst.dma(s.dma_start(out=ydram, in_=Y_sb[:, :]))
    b_end_pe = pe.n
    b_end_act = act.n
    if upto == "Cy":
        dump("d_ydram", ydram, [128, NTILE_MY * D], BF16)
        finish(recs, dumps, [sq, gq, pe, act, dve, syst])
        replay()
        return nc
    bC.close()
    kvY.close()

    xn2T = ctx.enter_context(nc.sbuf_tensor("xn2T", [128, NCB * TPC], BF16))
    cw = ExitStack()
    wo_sb = cw.enter_context(nc.sbuf_tensor("wo_sb", [128, NCB * D], BF16))
    ytile = [cw.enter_context(nc.sbuf_tensor(f"ytile{i}", [128, D], BF16)) for i in range(2)]
    YTt = cw.enter_context(nc.sbuf_tensor("YTt", [128, D], BF16))
    xa2 = [cw.enter_context(nc.sbuf_tensor(f"xa2_{i}", [128, D], F32)) for i in range(2)]
    x2_sb = cw.enter_context(nc.sbuf_tensor("x2_sb", [128, D], F32))
    xn2b = cw.enter_context(nc.sbuf_tensor("xn2b", [128, D], BF16))
    sq2 = cw.enter_context(nc.sbuf_tensor("sq2", [128, D], BF16))
    st2 = cw.enter_context(nc.sbuf_tensor("st2", [128, 8], F32))
    wops = cw.enter_context(nc.psum_tensor("wops", [128, D], F32))
    ctp = cw.enter_context(nc.psum_tensor("ctp", [128, 1024], BF16))

    @on("gpsimd")
    def _(g):
        syst.wait(g)
        pe.wait(g, b_end_pe)
        act.wait(g, b_end_act)
        dve.wait(g, b_end_dve)
        for cq in range(4):
            swo.dma(g.dma_start(
                out=wo_sb[:, cq * 4 * D:(cq + 1) * 4 * D].rearrange("p (c o) -> p c o", c=4),
                in_=woT.rearrange("(c p) o -> p c o", p=128)[:, cq * 4:(cq + 1) * 4, :]))

    @on("sync")
    def _(s):
        syst.wait(s)
        pe.wait(s, b_end_pe)
        dve.wait(s, b_end_dve)
        for i in range(2):
            syy[i].dma(s.dma_start(out=ytile[i][:, :], in_=ydram[:, i * D:(i + 1) * D]))
            sqx[i].dma(s.dma_start(out=xa2[i][:, :], in_=x_my[i * 128:(i + 1) * 128, :]))
    c_y0 = [syy[0].n, syy[1].n]
    c_x0 = [sqx[0].n, sqx[1].n]

    cstate = dict(ytile_free=[0, 0], xa2_free=[0, 0], YTt_free=0, ctp_free=0,
                  wops_free=dve.n, x2_free=0, xn2b_free=0)
    for t in range(NTILE_MY):
        par = t % 2
        for r in range(2):
            @on("tensor")
            def _(te):
                syy[par].wait(te, c_y0[par] + 16 * (t // 2))
                if t == 0 and r == 0:
                    dve.wait(te, b_end_dve)
                dve.wait(te, cstate["ctp_free"])
                if r == 0:
                    pe.wait(te, cstate["YTt_free"])
                for j in range(8):
                    c = r * 8 + j
                    pe.inc(te.transpose(ctp[:, j * 128:(j + 1) * 128],
                                               ytile[par][:, c * 128:(c + 1) * 128], eye_sb[:, :]))
            tr_done = pe.n

            @on("vector")
            def _(v):
                dve.wait(v)
                pe.wait(v, tr_done)
                dve.inc(v.tensor_copy(YTt[:, r * 1024:(r + 1) * 1024], ctp[:, :]))
            cstate["ctp_free"] = dve.n
        cstate["ytile_free"][par] = pe.n
        yt_ready = dve.n

        if t + 2 < NTILE_MY:
            @on("sync")
            def _(s):
                pe.wait(s, cstate["ytile_free"][par])
                syy[par].dma(s.dma_start(out=ytile[par][:, :], in_=ydram[:, (t + 2) * D:(t + 3) * D]))

        @on("tensor")
        def _(te):
            dve.wait(te, yt_ready)
            if t == 0:
                swo.wait(te)
            dve.wait(te, cstate["wops_free"])
            for cb in range(NCB):
                lhs = YTt[:, cb * 128:(cb + 1) * 128]
                for oc in range(4):
                    pe.inc(te.matmul(wops[:, oc * 512:(oc + 1) * 512], lhs,
                                            wo_sb[:, cb * D + oc * 512: cb * D + (oc + 1) * 512],
                                            start=(cb == 0), stop=(cb == NCB - 1)))
        wo_done = pe.n
        cstate["YTt_free"] = pe.n

        @on("scalar")
        def _(a):
            act.wait(a)
            pe.wait(a, wo_done)
            act.inc(a.activation(sq2[:, :], wops[:, :], AF.Square, accum_out=st2[:, 1:2]))
        c_ssq = act.n

        @on("vector")
        def _(v):
            dve.wait(v)
            act.wait(v, c_ssq)
            dve.inc(v.tensor_reduce(st2[:, 0:1], wops[:, :], AX.X, ALU.add))
            dve.wait(v)
            dve.inc(v.tensor_tensor(st2[:, 2:3], st2[:, 0:1], st2[:, 0:1], ALU.mult))
            dve.wait(v)
            dve.inc(v.scalar_tensor_tensor(st2[:, 2:3], st2[:, 2:3], -1.0 / D, st2[:, 1:2], ALU.mult, ALU.add))
        c_var = dve.n

        @on("scalar")
        def _(a):
            act.wait(a)
            dve.wait(a, c_var)
            act.inc(a.activation(st2[:, 4:5], st2[:, 2:3], AF.Ln, scale=1.0 / D,
                                 bias=consts_sb[:, C_EPS_LN:C_EPS_LN + 1]))
            act.wait(a)
            act.inc(a.activation(st2[:, 3:4], st2[:, 4:5], AF.Exp, scale=-0.5))
        c_rstd = act.n

        @on("vector")
        def _(v):
            dve.wait(v)
            act.wait(v, c_rstd)
            sqx[par].wait(v, c_x0[par] + 16 * (t // 2))
            dve.inc(v.tensor_tensor(st2[:, 5:6], st2[:, 0:1], st2[:, 3:4], ALU.mult))
            dve.wait(v)
            dve.inc(v.tensor_scalar(st2[:, 5:6], st2[:, 5:6], -1.0 / D, None, ALU.mult))
            dve.wait(v)
            sx2st.wait(v, 16 * t)
            dve.inc(v.tensor_scalar(xa2[par][:, :], xa2[par][:, :], 1.0, st2[:, 5:6], ALU.mult, ALU.add))
            dve.wait(v)
            dve.inc(v.scalar_tensor_tensor(x2_sb[:, :], wops[:, :], st2[:, 3:4], xa2[par][:, :],
                                           ALU.mult, ALU.add))
        cstate["wops_free"] = dve.n
        cstate["xa2_free"][par] = dve.n
        x2_ready = dve.n

        @on("sync")
        def _(s):
            dve.wait(s, x2_ready)
            sx2st.dma(s.dma_start(out=out[t * 128:(t + 1) * 128, :], in_=x2_sb[:, :]))
            if t + 2 < NTILE_MY:
                dve.wait(s, cstate["xa2_free"][par])
                sqx[par].dma(s.dma_start(out=xa2[par][:, :], in_=x_my[(t + 2) * 128:(t + 3) * 128, :]))

        @on("scalar")
        def _(a):
            act.wait(a)
            dve.wait(a, x2_ready)
            act.inc(a.activation(sq2[:, :], x2_sb[:, :], AF.Square, accum_out=st2[:, 1:2]))
        c2_ssq = act.n

        @on("vector")
        def _(v):
            dve.wait(v)
            act.wait(v, c2_ssq)
            dve.inc(v.tensor_reduce(st2[:, 0:1], x2_sb[:, :], AX.X, ALU.add))
            dve.wait(v)
            dve.inc(v.tensor_tensor(st2[:, 2:3], st2[:, 0:1], st2[:, 0:1], ALU.mult))
            dve.wait(v)
            dve.inc(v.scalar_tensor_tensor(st2[:, 2:3], st2[:, 2:3], -1.0 / D, st2[:, 1:2], ALU.mult, ALU.add))
        c2_var = dve.n

        @on("scalar")
        def _(a):
            act.wait(a)
            dve.wait(a, c2_var)
            act.inc(a.activation(st2[:, 4:5], st2[:, 2:3], AF.Ln, scale=1.0 / D,
                                 bias=consts_sb[:, C_EPS_LN:C_EPS_LN + 1]))
            act.wait(a)
            act.inc(a.activation(st2[:, 3:4], st2[:, 4:5], AF.Exp, scale=-0.5))
        c2_rstd = act.n

        @on("vector")
        def _(v):
            dve.wait(v)
            act.wait(v, c2_rstd)
            dve.inc(v.tensor_tensor(st2[:, 5:6], st2[:, 0:1], st2[:, 3:4], ALU.mult))
            dve.wait(v)
            dve.inc(v.tensor_scalar(st2[:, 5:6], st2[:, 5:6], -1.0 / D, None, ALU.mult))
            pe.wait(v, cstate["xn2b_free"])
            dve.wait(v)
            dve.inc(v.tensor_scalar(xn2b[:, :], x2_sb[:, :], st2[:, 3:4], st2[:, 5:6], ALU.mult, ALU.add))
        cstate["x2_free"] = dve.n
        xn2_ready = dve.n

        for r in range(2):
            @on("tensor")
            def _(te):
                dve.wait(te, xn2_ready)
                dve.wait(te, cstate["ctp_free"])
                for j in range(8):
                    c = r * 8 + j
                    pe.inc(te.transpose(ctp[:, j * 128:(j + 1) * 128],
                                               xn2b[:, c * 128:(c + 1) * 128], eye_sb[:, :]))
            tr2_done = pe.n

            @on("vector")
            def _(v):
                dve.wait(v)
                pe.wait(v, tr2_done)
                dve.inc(v.tensor_copy(
                    xn2T[:, :].rearrange("p (c q) -> p c q", c=NCB)[:, r * 8:(r + 1) * 8, t * 128:(t + 1) * 128],
                    ctp[:, :].rearrange("p (j q) -> p j q", j=8)))
            cstate["ctp_free"] = dve.n
        cstate["xn2b_free"] = pe.n

    if upto == "C":
        dump("d_ytt", YTt[:, :], [128, D], BF16)
        dump("d_x2sb", x2_sb[:, :], [128, D], F32)
        dump("d_xn2t", xn2T[:, :], [128, NCB * TPC], BF16)
        finish(recs, dumps, [sq, gq, pe, act, dve])
        replay()
        return nc

    c_end_pe = pe.n
    c_end_dve = dve.n
    c_end_act = act.n
    cw.close()

    hid = ctx.enter_context(nc.sbuf_tensor("hid", [128, NDFF * TPC], BF16))
    dC = ExitStack()
    vbg_sb = dC.enter_context(nc.sbuf_tensor("vbg_sb", [72, DFF], BF16))
    vbu_sb = dC.enter_context(nc.sbuf_tensor("vbu_sb", [72, DFF], BF16))
    udn_sb = dC.enter_context(nc.sbuf_tensor("udn_sb", [128, NDFF * 72], BF16))
    vbd_sb = dC.enter_context(nc.sbuf_tensor("vbd_sb", [72, D], BF16))
    w72_sb = dC.enter_context(nc.sbuf_tensor("w72_sb", [72, TPC], BF16))
    tgs_sb = dC.enter_context(nc.sbuf_tensor("tgs_sb", [72, TPC], BF16))
    tus_sb = dC.enter_context(nc.sbuf_tensor("tus_sb", [72, TPC], BF16))
    t2s_sb = dC.enter_context(nc.sbuf_tensor("t2s_sb", [72, TPC], BF16))
    dPre = ExitStack()
    rupa_sb = dPre.enter_context(nc.sbuf_tensor("rupa_sb", [128, NCB * 72], BF16))
    rupr_sb = dPre.enter_context(nc.sbuf_tensor("rupr_sb", [128, NCB * 8], BF16))
    rupb_sb = dPre.enter_context(nc.sbuf_tensor("rupb_sb", [128, NCB * 72], BF16))
    msel_sb = dPre.enter_context(nc.sbuf_tensor("msel_sb", [9, 72], BF16))
    ebias_sb = dPre.enter_context(nc.sbuf_tensor("ebias_sb", [1, 8], F32))
    ebias_bc = dPre.enter_context(nc.sbuf_tensor("ebias_bc", [128, 8], F32))
    logit_sb = dPre.enter_context(nc.sbuf_tensor("logit_sb", [128, 8], F32))
    rt8_sb = dPre.enter_context(nc.sbuf_tensor("rt8_sb", [8, TPC], F32))
    w9_sb = dPre.enter_context(nc.sbuf_tensor("w9_sb", [128, 16], F32))
    rst = dPre.enter_context(nc.sbuf_tensor("rst", [128, 8], F32))
    w9b_sb = dPre.enter_context(nc.sbuf_tensor("w9b_sb", [128, 9], BF16))
    wext_sb = dPre.enter_context(nc.sbuf_tensor("wext_sb", [9, TPC], BF16))

    @on("gpsimd")
    def _(g):
        pe.wait(g, c_end_pe)
        dve.wait(g, c_end_dve)
        act.wait(g, c_end_act)
        sx2st.wait(g)
        srup.dma(g.dma_start(out=rupa_sb[:, :].rearrange("p (c o) -> p c o", c=NCB),
                             in_=rupa.rearrange("(c p) o -> p c o", p=128)))
        srup.dma(g.dma_start(out=rupr_sb[:, :].rearrange("p (c o) -> p c o", c=NCB),
                             in_=rupr.rearrange("(c p) o -> p c o", p=128)))
        srup.dma(g.dma_start(out=rupb_sb[:, :].rearrange("p (c o) -> p c o", c=NCB),
                             in_=rupb.rearrange("(c p) o -> p c o", p=128)))
        srup.dma(g.dma_start(out=vbg_sb[:, :], in_=vbg))
        srup.dma(g.dma_start(out=vbu_sb[:, :], in_=vbu))
        srup.dma(g.dma_start(out=udn_sb[:, :].rearrange("p (n r) -> p n r", n=NDFF),
                             in_=udn.rearrange("(n p) r -> p n r", p=128)))
        srup.dma(g.dma_start(out=vbd_sb[:, :], in_=vbd))
        srup.dma(g.dma_start(out=msel_sb[:, :], in_=msel))
        srup.dma(g.dma_start(out=ebias_sb[:, :], in_=ebias))
    n_rup = srup.n

    dP0 = ExitStack()
    rupr_ps = dP0.enter_context(nc.psum_tensor("rupr_ps", [8, TPC], F32))
    eb_ps = dP0.enter_context(nc.psum_tensor("eb_ps", [128, 8], F32))

    @on("tensor")
    def _(te):
        srup.wait(te, n_rup)
        dve.wait(te, c_end_dve)
        pe.inc(te.matmul(eb_ps[:, :], onerf_sb[:, :], ebias_sb[:, :], start=True, stop=True))
        for nh in range(2):
            for cb in range(NCB):
                pe.inc(te.matmul(rupr_ps[:, nh * 512:(nh + 1) * 512],
                                        rupr_sb[:, cb * 8:(cb + 1) * 8],
                                        xn2T[:, cb * TPC + nh * 512: cb * TPC + (nh + 1) * 512],
                                        start=(cb == 0), stop=(cb == NCB - 1)))
    r_mm = pe.n

    @on("vector")
    def _(v):
        dve.wait(v)
        pe.wait(v, r_mm)
        dve.inc(v.tensor_copy(ebias_bc[:, :], eb_ps[:, :]))
        dve.inc(v.tensor_copy(rt8_sb[:, :], rupr_ps[:, :]))
    rt8_done = dve.n
    dP0.close()

    dP = ExitStack()
    tra_ps = dP.enter_context(nc.psum_tensor("tra_ps", [72, TPC], F32))
    trb_ps = dP.enter_context(nc.psum_tensor("trb_ps", [72, TPC], F32))
    w72_ps = dP.enter_context(nc.psum_tensor("w72_ps", [72, TPC], F32))
    rt_ps = dP.enter_context(nc.psum_tensor("rt_ps", [128, 8], F32))
    wext_ps = dP.enter_context(nc.psum_tensor("wext_ps", [9, TPC], BF16))

    @on("tensor")
    def _(te):
        dve.wait(te, rt8_done)
        for nh in range(2):
            for cb in range(NCB):
                pe.inc(te.matmul(tra_ps[:, nh * 512:(nh + 1) * 512],
                                        rupa_sb[:, cb * 72:(cb + 1) * 72],
                                        xn2T[:, cb * TPC + nh * 512: cb * TPC + (nh + 1) * 512],
                                        start=(cb == 0), stop=(cb == NCB - 1)))
            for cb in range(NCB):
                pe.inc(te.matmul(trb_ps[:, nh * 512:(nh + 1) * 512],
                                        rupb_sb[:, cb * 72:(cb + 1) * 72],
                                        xn2T[:, cb * TPC + nh * 512: cb * TPC + (nh + 1) * 512],
                                        start=(cb == 0), stop=(cb == NCB - 1)))
    dpre_mm = pe.n

    for t in range(NTILE_MY):
        @on("tensor")
        def _(te):
            dve.wait(te, rt8_done)
            if t > 0:
                dve.wait(te, rt_evac)
            pe.inc(te.transpose(rt_ps[:, :], rt8_sb[:, t * 128:(t + 1) * 128], eyef8_sb[:, :]))
        rt_tr = pe.n

        @on("vector")
        def _(v):
            dve.wait(v)
            pe.wait(v, rt_tr)
            dve.inc(v.tensor_copy(logit_sb[:, :], rt_ps[:, :]))
        rt_evac = dve.n

        @on("scalar")
        def _(a):
            act.wait(a)
            dve.wait(a, rt_evac)
            act.inc(a.activation(w9_sb[:, 0:8], logit_sb[:, :], AF.Sigmoid))
        sig_done = act.n

        @on("vector")
        def _(v):
            dve.wait(v)
            act.wait(v, sig_done)
            dve.inc(v.tensor_tensor(w9_sb[:, 8:16], w9_sb[:, 0:8], ebias_bc[:, :], ALU.add))
            dve.wait(v)
            dve.inc(v.tensor_reduce(rst[:, 0:1], w9_sb[:, 8:16], AX.X, ALU.max))
            dve.wait(v)
            dve.inc(v.tensor_scalar(logit_sb[:, :], w9_sb[:, 8:16], rst[:, 0:1], None, ALU.is_equal))
            dve.wait(v)
            dve.inc(v.scalar_tensor_tensor(logit_sb[:, :], logit_sb[:, :], -1e30, w9_sb[:, 8:16],
                                           ALU.mult, ALU.add))
            dve.wait(v)
            dve.inc(v.tensor_reduce(rst[:, 1:2], logit_sb[:, :], AX.X, ALU.max))
            dve.wait(v)
            dve.inc(v.tensor_scalar(logit_sb[:, :], w9_sb[:, 8:16], rst[:, 1:2], None, ALU.is_ge))
            dve.wait(v)
            dve.inc(v.tensor_tensor(logit_sb[:, :], logit_sb[:, :], w9_sb[:, 0:8], ALU.mult))
            dve.wait(v)
            dve.inc(v.tensor_reduce(rst[:, 2:3], logit_sb[:, :], AX.X, ALU.add))
            dve.wait(v)
            dve.inc(v.tensor_scalar(rst[:, 2:3], rst[:, 2:3], 1e-8, None, ALU.add))
            dve.wait(v)
            dve.inc(v.reciprocal(rst[:, 3:4], rst[:, 2:3]))
            dve.wait(v)
            dve.inc(v.tensor_scalar(w9b_sb[:, 0:8], logit_sb[:, :], rst[:, 3:4], None, ALU.mult))
            dve.wait(v)
            dve.inc(v.memset(w9b_sb[:, 8:9], 1.0))
        w9_done = dve.n

        @on("tensor")
        def _(te):
            dve.wait(te, w9_done)
            pe.inc(te.transpose(wext_ps[0:9, t * 128:(t + 1) * 128],
                                       w9b_sb[:, 0:9], eye_sb[:, :]))
        w9_tr = pe.n

        @on("vector")
        def _(v):
            dve.wait(v)
            pe.wait(v, w9_tr)
            dve.inc(v.tensor_copy(wext_sb[:, t * 128:(t + 1) * 128], wext_ps[0:9, t * 128:(t + 1) * 128]))
        rt_evac = dve.n
    wext_done = dve.n

    @on("tensor")
    def _(te):
        dve.wait(te, wext_done)
        for nh in range(2):
            pe.inc(te.matmul(w72_ps[:, nh * 512:(nh + 1) * 512], msel_sb[:, :],
                                    wext_sb[:, nh * 512:(nh + 1) * 512], start=True, stop=True))
    w72_mm = pe.n

    @on("vector")
    def _(v):
        dve.wait(v)
        pe.wait(v, w72_mm)
        pe.wait(v, dpre_mm)
        dve.inc(v.tensor_copy(w72_sb[:, :], w72_ps[:, :]))
        dve.wait(v)
        dve.inc(v.tensor_tensor(tgs_sb[:, :], tra_ps[:, :], w72_sb[:, :], ALU.mult))
        dve.wait(v)
        dve.inc(v.tensor_tensor(tus_sb[:, :], trb_ps[:, :], w72_sb[:, :], ALU.mult))
    dpre_done = dve.n

    dP.close()
    dPre.close()

    d1S = ExitStack()
    wg_sb = [d1S.enter_context(nc.sbuf_tensor(f"wg_sb{i}", [128, D], BF16)) for i in range(2)]
    wu_sb = [d1S.enter_context(nc.sbuf_tensor(f"wu_sb{i}", [128, D], BF16)) for i in range(2)]
    sg_sb = d1S.enter_context(nc.sbuf_tensor("sg_sb", [128, 512], BF16))
    d1P = ExitStack()
    gps = [d1P.enter_context(nc.psum_tensor(f"gps{i}", [128, 512], F32)) for i in range(2)]
    ups = [d1P.enter_context(nc.psum_tensor(f"ups{i}", [128, 512], F32)) for i in range(2)]
    t2_ps = d1P.enter_context(nc.psum_tensor("t2_ps", [72, TPC], F32))

    @on("gpsimd")
    def _(g):
        dve.wait(g, dpre_done)
        for i in range(2):
            swg[i].dma(g.dma_start(out=wg_sb[i][:, :].rearrange("p (c o) -> p c o", c=NCB),
                                   in_=wg_r[i].rearrange("(c p) o -> p c o", p=128)))
            swg[i].dma(g.dma_start(out=wu_sb[i][:, :].rearrange("p (c o) -> p c o", c=NCB),
                                   in_=wu_r[i].rearrange("(c p) o -> p c o", p=128)))
    d1_w0 = [swg[0].n, swg[1].n]

    dstate = dict(g_free=[act.n, act.n], u_free=[dve.n, dve.n],
                  sg_free=dve.n, w_free=[0, 0])
    pend_t2 = [None]
    for dffb in range(NDFF):
        wpar = dffb % 2
        for nh in range(2):
            par = nh
            if pend_t2[0] is not None:
                pend_t2[0]()
                pend_t2[0] = None

            @on("tensor")
            def _(te):
                if dffb == 0 and nh == 0:
                    dve.wait(te, dpre_done)
                swg[wpar].wait(te, d1_w0[wpar] + 32 * (dffb // 2))
                act.wait(te, dstate["g_free"][par])
                dve.wait(te, dstate["u_free"][par])
                for cb in range(NCB):
                    pe.inc(te.matmul(gps[par][:, :], wg_sb[wpar][:, cb * 128:(cb + 1) * 128],
                                            xn2T[:, cb * TPC + nh * 512: cb * TPC + (nh + 1) * 512],
                                            start=(cb == 0), stop=False))
                pe.inc(te.matmul(gps[par][:, :], vbg_sb[:, dffb * 128:(dffb + 1) * 128],
                                        tgs_sb[:, nh * 512:(nh + 1) * 512], start=False, stop=True))
                for cb in range(NCB):
                    pe.inc(te.matmul(ups[par][:, :], wu_sb[wpar][:, cb * 128:(cb + 1) * 128],
                                            xn2T[:, cb * TPC + nh * 512: cb * TPC + (nh + 1) * 512],
                                            start=(cb == 0), stop=False))
                pe.inc(te.matmul(ups[par][:, :], vbu_sb[:, dffb * 128:(dffb + 1) * 128],
                                        tus_sb[:, nh * 512:(nh + 1) * 512], start=False, stop=True))
            gu_done = pe.n

            @on("scalar")
            def _(a):
                act.wait(a)
                pe.wait(a, gu_done)
                dve.wait(a, dstate["sg_free"])
                act.inc(a.activation(sg_sb[:, :], gps[par][:, :], AF.Silu))
            silu_done = act.n
            dstate["g_free"][par] = act.n

            @on("vector")
            def _(v):
                dve.wait(v)
                act.wait(v, silu_done)
                dve.inc(v.tensor_tensor(hid[:, dffb * TPC + nh * 512: dffb * TPC + (nh + 1) * 512],
                                        sg_sb[:, :], ups[par][:, :], ALU.mult))
            dstate["u_free"][par] = dve.n
            dstate["sg_free"] = dve.n
            hid_done = dve.n

            def t2_emit(dffb=dffb, nh=nh, hid_done=hid_done):
                @on("tensor")
                def _(te):
                    dve.wait(te, hid_done)
                    pe.inc(te.matmul(t2_ps[:, nh * 512:(nh + 1) * 512],
                                            udn_sb[:, dffb * 72:(dffb + 1) * 72],
                                            hid[:, dffb * TPC + nh * 512: dffb * TPC + (nh + 1) * 512],
                                            start=(dffb == 0), stop=(dffb == NDFF - 1)))
            pend_t2[0] = t2_emit

        if dffb + 2 < NDFF:
            @on("gpsimd")
            def _(g):
                pe.wait(g, pe.n)
                swg[wpar].dma(g.dma_start(out=wg_sb[wpar][:, :].rearrange("p (c o) -> p c o", c=NCB),
                                          in_=wg_r[dffb + 2].rearrange("(c p) o -> p c o", p=128)))
                swg[wpar].dma(g.dma_start(out=wu_sb[wpar][:, :].rearrange("p (c o) -> p c o", c=NCB),
                                          in_=wu_r[dffb + 2].rearrange("(c p) o -> p c o", p=128)))
    pend_t2[0]()
    pend_t2[0] = None
    d1_mm = pe.n

    @on("vector")
    def _(v):
        dve.wait(v)
        pe.wait(v, d1_mm)
        dve.inc(v.tensor_tensor(t2s_sb[:, :], t2_ps[:, :], w72_sb[:, :], ALU.mult))
    t2s_done = dve.n

    if upto == "D1":
        dump("d_hid", hid[:, 0:4 * TPC], [128, 4 * TPC], BF16)
        dump("d_t2s", t2s_sb[:, :], [72, TPC], BF16)
        dump("d_w72", w72_sb[:, :], [72, TPC], BF16)
        finish(recs, dumps, [sq, gq, pe, act, dve])
        replay()
        return nc

    d1P.close()
    d1S.close()

    d2P = ExitStack()
    ops = [d2P.enter_context(nc.psum_tensor(f"ops{i}", [128, 512], F32)) for i in range(NTILE_MY)]
    wdq_sb = [dC.enter_context(nc.sbuf_tensor(f"wdq_sb{i}", [128, 4 * 512], BF16)) for i in range(2)]
    ffn_sb = xn2T
    x2r_sb = dC.enter_context(nc.sbuf_tensor("x2r_sb", [128, D], F32))
    out_sb = dC.enter_context(nc.sbuf_tensor("out_sb", [128, D], F32))

    NQ = 4

    @on("gpsimd")
    def _(g):
        dve.wait(g, t2s_done)
        for i in range(2):
            swd[i].dma(g.dma_start(
                out=wdq_sb[i][:, :].rearrange("p (c o) -> p c o", c=4),
                in_=wd[i * 512:(i + 1) * 512, 0:512].rearrange("(c p) o -> p c o", p=128)))
    wd_loaded = {0: swd[0].n, 1: swd[1].n}

    evac_done = [dve.n] * NTILE_MY
    dn_count = 0
    for j in range(NQ):
        for ch in range(11):
            dffb0 = ch * 4
            nd = min(4, NDFF - dffb0)
            cpar = (j * 11 + ch) % 2

            @on("tensor")
            def _(te):
                swd[cpar].wait(te, wd_loaded[j * 11 + ch])
                if j == 0 and ch == 0:
                    dve.wait(te, t2s_done)
                for dd in range(nd):
                    dffb = dffb0 + dd
                    for qt in range(NTILE_MY):
                        if dffb == 0:
                            dve.wait(te, evac_done[qt])
                        pe.inc(te.matmul(
                            ops[qt][:, :],
                            hid[:, dffb * TPC + qt * 128: dffb * TPC + (qt + 1) * 128],
                            wdq_sb[cpar][:, dd * 512:(dd + 1) * 512],
                            start=(dffb == 0), stop=False))
            dstate["w_free"][cpar] = pe.n

            nxt = j * 11 + ch + 2
            if nxt < NQ * 11:
                nj, nch = divmod(nxt, 11)
                ndffb0 = nch * 4
                nnd = min(4, NDFF - ndffb0)
                @on("gpsimd")
                def _(g):
                    pe.wait(g, dstate["w_free"][nxt % 2])
                    swd[nxt % 2].dma(g.dma_start(
                        out=wdq_sb[nxt % 2][:, 0:nnd * 512].rearrange("p (c o) -> p c o", c=nnd),
                        in_=wd[ndffb0 * 128:(ndffb0 + nnd) * 128, nj * 512:(nj + 1) * 512].rearrange("(c p) o -> p c o", p=128)))
                wd_loaded[nxt] = swd[nxt % 2].n

        for qt in range(NTILE_MY):
            @on("tensor")
            def _(te):
                pe.inc(te.matmul(ops[qt][:, :], t2s_sb[:, qt * 128:(qt + 1) * 128],
                                        vbd_sb[:, j * 512:(j + 1) * 512], start=False, stop=True))
            q_done = pe.n

            @on("vector")
            def _(v):
                dve.wait(v)
                pe.wait(v, q_done)
                dve.inc(v.tensor_copy(ffn_sb[:, qt * D + j * 512: qt * D + (j + 1) * 512], ops[qt][:, :]))
            evac_done[qt] = dve.n

    d2_end = dve.n
    x2r_free = dve.n
    out_free = dve.n
    sq2d = dC.enter_context(nc.sbuf_tensor("sq2d", [128, D], BF16))
    std_ = dC.enter_context(nc.sbuf_tensor("std_", [128, 8], F32))
    for t in range(NTILE_MY):
        @on("sync")
        def _(s):
            sx2st.wait(s)
            dve.wait(s, x2r_free)
            sx2ld.dma(s.dma_start(out=x2r_sb[:, :], in_=out[t * 128:(t + 1) * 128, :]))
        x2_ld = sx2ld.n

        @on("scalar")
        def _(a):
            act.wait(a)
            dve.wait(a, evac_done[t])
            act.inc(a.activation(sq2d[:, :], ffn_sb[:, t * D:(t + 1) * D], AF.Square, accum_out=std_[:, 1:2]))
        d_ssq = act.n

        @on("vector")
        def _(v):
            dve.wait(v)
            act.wait(v, d_ssq)
            dve.inc(v.tensor_reduce(std_[:, 0:1], ffn_sb[:, t * D:(t + 1) * D], AX.X, ALU.add))
            dve.wait(v)
            dve.inc(v.tensor_tensor(std_[:, 2:3], std_[:, 0:1], std_[:, 0:1], ALU.mult))
            dve.wait(v)
            dve.inc(v.scalar_tensor_tensor(std_[:, 2:3], std_[:, 2:3], -1.0 / D, std_[:, 1:2], ALU.mult, ALU.add))
        d_var = dve.n

        @on("scalar")
        def _(a):
            act.wait(a)
            dve.wait(a, d_var)
            act.inc(a.activation(std_[:, 4:5], std_[:, 2:3], AF.Ln, scale=1.0 / D,
                                 bias=consts_sb[:, C_EPS_LN:C_EPS_LN + 1]))
            act.wait(a)
            act.inc(a.activation(std_[:, 3:4], std_[:, 4:5], AF.Exp, scale=-0.5))
        d_rstd = act.n

        @on("vector")
        def _(v):
            dve.wait(v)
            act.wait(v, d_rstd)
            sx2ld.wait(v, x2_ld)
            dve.inc(v.tensor_tensor(std_[:, 5:6], std_[:, 0:1], std_[:, 3:4], ALU.mult))
            dve.wait(v)
            dve.inc(v.tensor_scalar(std_[:, 5:6], std_[:, 5:6], -1.0 / D, None, ALU.mult))
            dve.wait(v)
            dve.inc(v.tensor_scalar(x2r_sb[:, :], x2r_sb[:, :], 1.0, std_[:, 5:6], ALU.mult, ALU.add))
            dve.wait(v)
            sout.wait(v, 16 * t)
            dve.inc(v.scalar_tensor_tensor(out_sb[:, :], ffn_sb[:, t * D:(t + 1) * D], std_[:, 3:4],
                                           x2r_sb[:, :], ALU.mult, ALU.add))
        x2r_free = dve.n
        out_ready = dve.n

        @on("sync")
        def _(s):
            dve.wait(s, out_ready)
            sout.dma(s.dma_start(out=out[t * 128:(t + 1) * 128, :], in_=out_sb[:, :]))
        dn_count += 1

    finish(recs, dumps, [sq, gq, pe, act, dve, sout, sx2st, syst])
    replay()
    return nc


def finish(recs, dumps, sems):
    sq = sems[0]
    s = recs["gpsimd"]
    for sem in sems[1:]:
        sem.wait(s)
    for d, ap in dumps:
        sq.dma(s.dma_start(out=d, in_=ap))
    sq.wait(s)


def prep_inputs(I):
    f32 = np.float32
    x = np.ascontiguousarray(np.asarray(I["L0_attn_x"], dtype=f32).reshape(NT, D))
    wqT = np.ascontiguousarray(np.asarray(I["L0_attn_Wq"], f32).T)
    wkT = np.ascontiguousarray(np.asarray(I["L0_attn_Wk"], f32).T)
    wvT = np.ascontiguousarray(np.asarray(I["L0_attn_Wv"], f32).T)
    woT = np.ascontiguousarray(np.asarray(I["L0_attn_Wo"], f32).T)
    wg = np.asarray(I["L0_ffn_W_gate"], f32)
    wu = np.asarray(I["L0_ffn_W_up"], f32)
    wd = np.ascontiguousarray(np.asarray(I["L0_ffn_W_down"], f32))
    wg_r = np.ascontiguousarray(wg.reshape(NCB, 128, NDFF, 128).transpose(2, 1, 0, 3).reshape(NDFF, 128, D))
    wu_r = np.ascontiguousarray(wu.reshape(NCB, 128, NDFF, 128).transpose(2, 1, 0, 3).reshape(NDFF, 128, D))
    ug = np.concatenate([np.asarray(I["L0_ffn_sA_gate"], f32),
                         np.asarray(I["L0_ffn_A_gate"], f32).transpose(1, 0, 2).reshape(D, 64)], axis=1)
    uu = np.concatenate([np.asarray(I["L0_ffn_sA_up"], f32),
                         np.asarray(I["L0_ffn_A_up"], f32).transpose(1, 0, 2).reshape(D, 64)], axis=1)
    rupa = np.ascontiguousarray(ug)
    rupr = np.ascontiguousarray(np.asarray(I["L0_ffn_W_router"], f32))
    rupb = np.ascontiguousarray(uu)
    vbg = np.concatenate([np.asarray(I["L0_ffn_sB_gate"], f32),
                          np.asarray(I["L0_ffn_B_gate"], f32).reshape(64, DFF)], axis=0)
    vbu = np.concatenate([np.asarray(I["L0_ffn_sB_up"], f32),
                          np.asarray(I["L0_ffn_B_up"], f32).reshape(64, DFF)], axis=0)
    udn = np.concatenate([np.asarray(I["L0_ffn_sA_down"], f32),
                          np.asarray(I["L0_ffn_A_down"], f32).transpose(1, 0, 2).reshape(DFF, 64)], axis=1)
    vbd = np.concatenate([np.asarray(I["L0_ffn_sB_down"], f32),
                          np.asarray(I["L0_ffn_B_down"], f32).reshape(64, D)], axis=0)
    msel = np.zeros((9, 72), BF)
    msel[8, 0:8] = BF(0.125)
    for e in range(8):
        msel[e, 8 + 8 * e: 16 + 8 * e] = BF(0.125)
    ebias = np.ascontiguousarray(np.asarray(I["L0_ffn_expert_bias"], f32).reshape(1, 8))
    consts = np.zeros((128, 8), f32)
    consts[:, C_EPS_LN] = 1e-5
    consts[:, C_EPS_RMS] = RMS_EPS
    consts[:, C_QBIAS] = -0.5 * np.log(128.0)
    consts[:, C_VNEPS] = 1e-24
    common = dict(
        wqT=wqT, wkT=wkT, wvT=wvT, woT=woT, wg_r=wg_r, wu_r=wu_r, wd=wd,
        rupa=rupa, rupr=rupr, rupb=rupb, vbg=vbg, vbu=vbu, udn=udn, vbd=vbd, msel=msel,
        ebias=ebias, eye=np.eye(128, dtype=BF), eyef=np.eye(128, dtype=np.float32),
        eyef8=np.eye(8, dtype=np.float32),
        onec=np.ones((128, 1), BF), onecf=np.ones((128, 1), np.float32),
        onerf=np.ones((1, 128), np.float32), consts=consts,
    )
    in_maps = []
    for c in range(NCORE):
        m = dict(common)
        m["x_my"] = np.ascontiguousarray(np.concatenate([x[c * TPC:, :], x[:c * TPC, :]], axis=0))
        in_maps.append(m)
    return in_maps


def kernel(**inputs):
    in_maps = prep_inputs(inputs)
    nc = build("full")
    res = run_bass_kernel_spmd(nc, in_maps, core_ids=list(range(NCORE)))
    out = np.concatenate([np.asarray(r["out"]) for r in res.results], axis=0)
    return out.reshape(1, NT, D).astype(np.float32)


# version 31
# speedup vs baseline: 1.0095x; 1.0095x over previous
import numpy as np
import ml_dtypes
from contextlib import ExitStack

import concourse.bass as bass
import concourse.mybir as mybir
from concourse.bass_utils import run_bass_kernel_spmd

F32 = mybir.dt.float32
BF16 = mybir.dt.bfloat16
AF = mybir.ActivationFunctionType
ALU = mybir.AluOpType
AX = mybir.AxisListType
BF = ml_dtypes.bfloat16

D = 2048
NT = 8192
NCORE = 8
TPC = NT // NCORE
NTILE_MY = TPC // 128
NTILE_ALL = NT // 128
NH = 16
NKV = 4
HD = 128
KV = 512
DFF = 5504
NDFF = DFF // 128
NCB = D // 128

C_EPS_LN = 0
C_EPS_RMS = 1
C_QBIAS = 2
C_VNEPS = 4
B_HEADS = NH
RMS_EPS = float(np.finfo(np.float32).eps)


class Sem:
    def __init__(self, nc, ctx, name):
        self.h = ctx.enter_context(nc.semaphore(name))
        self.n = 0

    def inc(self, instr, k=1):
        instr.then_inc(self.h, k)
        self.n += k
        return self.n

    def dma(self, instr):
        return self.inc(instr, 16)

    def wait(self, eng, n=None):
        eng.wait_ge(self.h, self.n if n is None else n)


class RecInstr:
    __slots__ = ("incs",)

    def __init__(self):
        self.incs = []

    def then_inc(self, sem, k):
        self.incs.append((sem, k))
        return self


class RecEngine:

    def __init__(self, q):
        self._q = q

    def __getattr__(self, op):
        q = self._q

        def call(*a, **kw):
            ri = RecInstr()
            q.append((op, a, kw, ri))
            return ri

        return call


REPLAY_CHUNK = 100000


def build(upto="full"):
    nc = bass.Bass("TRN2", target_bir_lowering=False)

    x_my = nc.dram_tensor("x_my", [NT, D], F32, kind="ExternalInput").ap()
    wqT = nc.dram_tensor("wqT", [D, D], F32, kind="ExternalInput").ap()
    wkT = nc.dram_tensor("wkT", [D, KV], F32, kind="ExternalInput").ap()
    wvT = nc.dram_tensor("wvT", [D, KV], F32, kind="ExternalInput").ap()
    woT = nc.dram_tensor("woT", [D, D], F32, kind="ExternalInput").ap()
    wg_r = nc.dram_tensor("wg_r", [NDFF, 128, D], F32, kind="ExternalInput").ap()
    wu_r = nc.dram_tensor("wu_r", [NDFF, 128, D], F32, kind="ExternalInput").ap()
    wd = nc.dram_tensor("wd", [DFF, D], F32, kind="ExternalInput").ap()
    rupa = nc.dram_tensor("rupa", [D, 72], F32, kind="ExternalInput").ap()
    rupr = nc.dram_tensor("rupr", [D, 8], F32, kind="ExternalInput").ap()
    rupb = nc.dram_tensor("rupb", [D, 72], F32, kind="ExternalInput").ap()
    vbg = nc.dram_tensor("vbg", [72, DFF], F32, kind="ExternalInput").ap()
    vbu = nc.dram_tensor("vbu", [72, DFF], F32, kind="ExternalInput").ap()
    udn = nc.dram_tensor("udn", [DFF, 72], F32, kind="ExternalInput").ap()
    vbd = nc.dram_tensor("vbd", [72, D], F32, kind="ExternalInput").ap()
    msel = nc.dram_tensor("msel", [9, 72], BF16, kind="ExternalInput").ap()
    ebias = nc.dram_tensor("ebias", [1, 8], F32, kind="ExternalInput").ap()
    eye = nc.dram_tensor("eye", [128, 128], BF16, kind="ExternalInput").ap()
    eyef = nc.dram_tensor("eyef", [128, 128], F32, kind="ExternalInput").ap()
    eyef8 = nc.dram_tensor("eyef8", [8, 8], F32, kind="ExternalInput").ap()
    onec = nc.dram_tensor("onec", [128, 1], BF16, kind="ExternalInput").ap()
    onecf = nc.dram_tensor("onecf", [128, 1], F32, kind="ExternalInput").ap()
    onerf = nc.dram_tensor("onerf", [1, 128], F32, kind="ExternalInput").ap()
    consts = nc.dram_tensor("consts", [128, 8], F32, kind="ExternalInput").ap()

    out = nc.dram_tensor("out", [TPC, D], F32, kind="ExternalOutput").ap()
    qtd = nc.dram_tensor("qtd", [128, NH * TPC], BF16, kind="Internal").ap()
    vnd = nc.dram_tensor("vnd", [128, NTILE_MY * KV], BF16, kind="Internal").ap()
    ydram = nc.dram_tensor("ydram", [128, NTILE_MY * D], BF16, kind="Internal").ap()

    dumps = []

    def dump(name, ap, shape, dtype):
        d = nc.dram_tensor(name, shape, F32, kind="ExternalOutput").ap()
        dumps.append((d, ap))

    ctx = ExitStack()
    eye_sb = ctx.enter_context(nc.sbuf_tensor("eye_sb", [128, 128], BF16))
    eyef_sb = ctx.enter_context(nc.sbuf_tensor("eyef_sb2", [128, 128], F32))
    eyef8_sb = ctx.enter_context(nc.sbuf_tensor("eyef8_sb", [8, 8], F32))
    onec_sb = ctx.enter_context(nc.sbuf_tensor("onec_sb", [128, 1], BF16))
    onecf_sb = ctx.enter_context(nc.sbuf_tensor("onecf_sb", [128, 1], F32))
    onerf_sb = ctx.enter_context(nc.sbuf_tensor("onerf_sb", [1, 128], F32))
    consts_sb = ctx.enter_context(nc.sbuf_tensor("consts_sb", [128, 8], F32))

    sq = Sem(nc, ctx, "sq")
    gq = Sem(nc, ctx, "gq")
    sqx = [Sem(nc, ctx, "sqx0"), Sem(nc, ctx, "sqx1"), Sem(nc, ctx, "sqx2")]
    qs = [Sem(nc, ctx, "qs0"), Sem(nc, ctx, "qs1")]
    swv = Sem(nc, ctx, "swv")
    swq = Sem(nc, ctx, "swq")
    swk = Sem(nc, ctx, "swk")
    swo = Sem(nc, ctx, "swo")
    srup = Sem(nc, ctx, "srup")
    syst = Sem(nc, ctx, "syst")
    syy = [Sem(nc, ctx, "syy0"), Sem(nc, ctx, "syy1")]
    sx2st = Sem(nc, ctx, "sx2st")
    swg = [Sem(nc, ctx, "swg0"), Sem(nc, ctx, "swg1")]
    swd = [Sem(nc, ctx, "swd0"), Sem(nc, ctx, "swd1")]
    sx2ld = Sem(nc, ctx, "sx2ld")
    sout = Sem(nc, ctx, "sout")
    svn2 = [Sem(nc, ctx, "svn0"), Sem(nc, ctx, "svn1")]
    gpc = Sem(nc, ctx, "gpc")
    pe = Sem(nc, ctx, "pe")
    act = Sem(nc, ctx, "act")
    dve = Sem(nc, ctx, "dve")

    queues = {name: [] for name in ("sync", "gpsimd", "vector", "scalar", "tensor")}
    recs = {name: RecEngine(q) for name, q in queues.items()}

    def on(name):
        return lambda f: f(recs[name])

    def replay():
        B = ctx.enter_context(nc.Block())
        for name in ("sync", "gpsimd", "vector", "scalar", "tensor"):
            q = queues[name]
            for c0 in range(0, len(q), REPLAY_CHUNK):
                chunk = q[c0:c0 + REPLAY_CHUNK]

                def emit(eng, chunk=chunk):
                    for op, a, kw, ri in chunk:
                        instr = getattr(eng, op)(*a, **kw)
                        for sem, k in ri.incs:
                            instr.then_inc(sem, k)

                getattr(B, name)(emit)

    @on("gpsimd")
    def _(g):
        gq.dma(g.dma_start(out=eye_sb[:, :], in_=eye))
        gq.dma(g.dma_start(out=eyef_sb[:, :], in_=eyef))
        gq.dma(g.dma_start(out=eyef8_sb[:, :], in_=eyef8))
        gq.dma(g.dma_start(out=onec_sb[:, :], in_=onec))
        gq.dma(g.dma_start(out=onecf_sb[:, :], in_=onecf))
        gq.dma(g.dma_start(out=onerf_sb[:, :], in_=onerf))
        gq.dma(g.dma_start(out=consts_sb[:, :], in_=consts))
    n_consts = gq.n

    @on("vector")
    def _(v):
        gq.wait(v, n_consts)

    @on("scalar")
    def _(a):
        gq.wait(a, n_consts)

    @on("tensor")
    def _(te):
        gq.wait(te, n_consts)

    aC = ExitStack()
    wv_sb = aC.enter_context(nc.sbuf_tensor("wv_sb", [128, NCB * KV], BF16))
    wk_sb = aC.enter_context(nc.sbuf_tensor("wk_sb", [128, NCB * KV], BF16))
    xa = [aC.enter_context(nc.sbuf_tensor(f"xa{i}", [128, D], F32)) for i in range(3)]
    xn = [aC.enter_context(nc.sbuf_tensor(f"xn{i}", [128, D], BF16)) for i in range(2)]
    xnT = [aC.enter_context(nc.sbuf_tensor(f"xnT{i}", [128, D], BF16)) for i in range(2)]
    st8 = [aC.enter_context(nc.sbuf_tensor(f"st8_{i}", [128, 8], F32)) for i in range(2)]
    rv4 = [aC.enter_context(nc.sbuf_tensor(f"rv4_{i}", [128, 4], F32)) for i in range(2)]
    vn1 = [aC.enter_context(nc.sbuf_tensor(f"vn1_{i}", [128, KV], BF16)) for i in range(2)]
    rk4 = [aC.enter_context(nc.sbuf_tensor(f"rk4_{i}", [128, 4], F32)) for i in range(2)]
    khat = [aC.enter_context(nc.sbuf_tensor(f"khat{i}", [128, KV], BF16)) for i in range(2)]
    tp_ps = [aC.enter_context(nc.psum_tensor(f"tp_ps{i}", [128, 1024], BF16)) for i in range(2)]

    a0w = ExitStack()
    wq_sb = a0w.enter_context(nc.sbuf_tensor("wq_sb", [128, NCB * D], BF16))
    qtstage = a0w.enter_context(nc.sbuf_tensor("qtstage", [128, NH * TPC], BF16))
    qhat = a0w.enter_context(nc.sbuf_tensor("qhat", [128, 2 * D], BF16))

    @on("gpsimd")
    def _(g):
        swv.dma(g.dma_start(out=wv_sb[:, :].rearrange("p (c o) -> p c o", c=NCB),
                            in_=wvT.rearrange("(c p) o -> p c o", p=128)))
        swq.dma(g.dma_start(out=wq_sb[:, :].rearrange("p (c o) -> p c o", c=NCB),
                            in_=wqT.rearrange("(c p) o -> p c o", p=128)))
        swk.dma(g.dma_start(out=wk_sb[:, :].rearrange("p (c o) -> p c o", c=NCB),
                            in_=wkT.rearrange("(c p) o -> p c o", p=128)))

    @on("sync")
    def _(s):
        for t in range(3):
            sqx[t].dma(s.dma_start(out=xa[t][:, :], in_=x_my[t * 128:(t + 1) * 128, :]))

    NQS = NTILE_MY * 2
    qps2 = [a0w.enter_context(nc.psum_tensor(f"qps2_{i}", [128, 1024], F32)) for i in range(2)]
    rq8 = [a0w.enter_context(nc.sbuf_tensor(f"rq8_{i}", [128, 8], F32)) for i in range(2)]

    qp = dict(sq={}, rstd={}, var={}, xn={}, tpe={}, tr={}, mm={}, qsq={}, qred={},
              qrms={}, qsc={}, qtr={}, qev={})

    def qslot(s):
        t, hf = divmod(s, 2)
        return hf * 2 + (t % 2)

    for s in range(-2, NQS + 6):
        if 0 <= s < NQS:
            t, hf = divmod(s, 2)

            @on("tensor")
            def _(te, s=s, t=t, hf=hf):
                dve.wait(te, qp["xn"][t])
                dve.wait(te, qp["tpe"].get(s - 1, 0))
                dve.wait(te, qp["qev"].get(s - 6, 0))
                for j in range(8):
                    c = hf * 8 + j
                    pe.inc(te.transpose(tp_ps[s % 2][:, j * 128:(j + 1) * 128],
                                        xn[t % 2][:, c * 128:(c + 1) * 128], eye_sb[:, :]))
            qp["tr"][s] = pe.n
        if 0 <= s - 5 < NQS:
            s3 = s - 5
            q3 = qslot(s3)

            @on("tensor")
            def _(te, s3=s3, q3=q3):
                dve.wait(te, qp["qsc"][s3])
                dve.wait(te, qp["qev"].get(s3 - 1, 0))
                dve.wait(te, qp["tpe"].get(s3 + 4, 0))
                for j in range(8):
                    pe.inc(te.transpose(tp_ps[s3 % 2][:, j * 128:(j + 1) * 128],
                                        qhat[:, q3 * 1024 + j * 128: q3 * 1024 + (j + 1) * 128],
                                        eye_sb[:, :]))
            qp["qtr"][s3] = pe.n
        if 0 <= s - 2 < NQS:
            s1 = s - 2
            t1, hf1 = divmod(s1, 2)

            @on("tensor")
            def _(te, s1=s1, t1=t1, hf1=hf1):
                if s1 == 0:
                    swq.wait(te)
                dve.wait(te, qp["tpe"][2 * t1 + 1])
                dve.wait(te, qp["qsc"].get(s1 - 2, 0))
                for cb in range(NCB):
                    lhs = xnT[t1 % 2][:, cb * 128:(cb + 1) * 128]
                    for oc in range(2):
                        o0 = hf1 * 1024 + oc * 512
                        pe.inc(te.matmul(qps2[s1 % 2][:, oc * 512:(oc + 1) * 512], lhs,
                                         wq_sb[:, cb * D + o0: cb * D + o0 + 512],
                                         start=(cb == 0), stop=(cb == NCB - 1)))
            qp["mm"][s1] = pe.n

        if 0 <= s - 3 < NQS:
            s2q = s - 3

            @on("scalar")
            def _(a, s2q=s2q):
                act.wait(a)
                dve.wait(a, qp["qred"][s2q])
                act.inc(a.activation(rq8[s2q % 2][:, :], rq8[s2q % 2][:, :], AF.Ln, scale=1.0 / HD,
                                     bias=consts_sb[:, C_EPS_RMS:C_EPS_RMS + 1]))
                act.wait(a)
                act.inc(a.activation(rq8[s2q % 2][:, :], rq8[s2q % 2][:, :], AF.Exp, scale=-0.5,
                                     bias=consts_sb[:, C_QBIAS:C_QBIAS + 1]))
            qp["qrms"][s2q] = act.n
        if 0 <= s + 2 < NQS and (s + 2) % 2 == 0:
            tq = (s + 2) // 2

            @on("scalar")
            def _(a, tq=tq):
                act.wait(a)
                sqx[tq % 3].wait(a, 16 * (tq // 3 + 1))
                pe.wait(a, qp["tr"].get(2 * (tq - 2) + 1, 0))
                act.inc(a.activation(xn[tq % 2][:, :], xa[tq % 3][:, :], AF.Square, accum_out=st8[tq % 2][:, 1:2]))
            qp["sq"][tq] = act.n
        if 0 <= s + 1 < NQS and (s + 1) % 2 == 0:
            tq = (s + 1) // 2

            @on("scalar")
            def _(a, tq=tq):
                act.wait(a)
                dve.wait(a, qp["var"][tq])
                act.inc(a.activation(st8[tq % 2][:, 4:5], st8[tq % 2][:, 2:3], AF.Ln, scale=1.0 / D,
                                     bias=consts_sb[:, C_EPS_LN:C_EPS_LN + 1]))
                act.wait(a)
                act.inc(a.activation(st8[tq % 2][:, 3:4], st8[tq % 2][:, 4:5], AF.Exp, scale=-0.5))
            qp["rstd"][tq] = act.n
        if 0 <= s - 2 < NQS:
            s1 = s - 2
            q1 = qslot(s1)

            @on("scalar")
            def _(a, s1=s1, q1=q1):
                act.wait(a)
                pe.wait(a, qp["mm"][s1])
                pe.wait(a, qp["qtr"].get(s1 - 4, 0))
                act.inc(a.activation(qhat[:, q1 * 1024:(q1 + 1) * 1024], qps2[s1 % 2][:, :], AF.Square))
            qp["qsq"][s1] = act.n

        if 0 <= s + 2 < NQS and (s + 2) % 2 == 0:
            tq = (s + 2) // 2

            @on("vector")
            def _(v, tq=tq):
                dve.wait(v)
                sqx[tq % 3].wait(v, 16 * (tq // 3 + 1))
                dve.inc(v.tensor_reduce(st8[tq % 2][:, 0:1], xa[tq % 3][:, :], AX.X, ALU.add))
                act.wait(v, qp["sq"][tq])
                dve.wait(v)
                dve.inc(v.tensor_tensor(st8[tq % 2][:, 2:3], st8[tq % 2][:, 0:1], st8[tq % 2][:, 0:1], ALU.mult))
                dve.wait(v)
                dve.inc(v.scalar_tensor_tensor(st8[tq % 2][:, 2:3], st8[tq % 2][:, 2:3], -1.0 / D,
                                               st8[tq % 2][:, 1:2], ALU.mult, ALU.add))
            qp["var"][tq] = dve.n
        if 0 <= s + 1 < NQS and (s + 1) % 2 == 0:
            tq = (s + 1) // 2

            @on("vector")
            def _(v, tq=tq):
                dve.wait(v)
                act.wait(v, qp["rstd"][tq])
                pe.wait(v, qp["tr"].get(2 * (tq - 1) + 1, 0))
                dve.inc(v.tensor_tensor(st8[tq % 2][:, 5:6], st8[tq % 2][:, 0:1], st8[tq % 2][:, 3:4], ALU.mult))
                dve.wait(v)
                dve.inc(v.tensor_scalar(st8[tq % 2][:, 5:6], st8[tq % 2][:, 5:6], -1.0 / D, None, ALU.mult))
                dve.wait(v)
                dve.inc(v.tensor_scalar(xn[tq % 2][:, :], xa[tq % 3][:, :], st8[tq % 2][:, 3:4],
                                        st8[tq % 2][:, 5:6], ALU.mult, ALU.add))
            qp["xn"][tq] = dve.n
        if 0 <= s < NQS:
            t, hf = divmod(s, 2)

            @on("vector")
            def _(v, s=s, t=t, hf=hf):
                dve.wait(v)
                pe.wait(v, qp["tr"][s])
                pe.wait(v, qp["mm"].get(2 * (t - 1) + 1, 0))
                dve.inc(v.tensor_copy(xnT[t % 2][:, hf * 1024:(hf + 1) * 1024], tp_ps[s % 2][:, :]))
            qp["tpe"][s] = dve.n
        if 0 <= s - 3 < NQS:
            s2q = s - 3
            q2 = qslot(s2q)

            @on("vector")
            def _(v, s2q=s2q, q2=q2):
                dve.wait(v)
                act.wait(v, qp["qrms"][s2q])
                for j in range(8):
                    dve.inc(v.tensor_scalar(qhat[:, q2 * 1024 + j * 128: q2 * 1024 + (j + 1) * 128],
                                            qps2[s2q % 2][:, j * 128:(j + 1) * 128],
                                            rq8[s2q % 2][:, j:j + 1], None, ALU.mult))
            qp["qsc"][s2q] = dve.n
        if 0 <= s - 5 < NQS:
            s3 = s - 5
            t3, hf3 = divmod(s3, 2)

            @on("vector")
            def _(v, s3=s3, t3=t3, hf3=hf3):
                dve.wait(v)
                pe.wait(v, qp["qtr"][s3])
                dve.inc(v.tensor_copy(
                    qtstage[:, :].rearrange("p (h q) -> p h q", h=NH)[:, hf3 * 8:(hf3 + 1) * 8,
                                                                     t3 * 128:(t3 + 1) * 128],
                    tp_ps[s3 % 2][:, :].rearrange("p (j q) -> p j q", j=8)))
            qp["qev"][s3] = dve.n
        if 0 <= s - 2 < NQS:
            s1 = s - 2
            q1 = qslot(s1)

            @on("vector")
            def _(v, s1=s1, q1=q1):
                dve.wait(v)
                act.wait(v, qp["qsq"][s1])
                dve.inc(v.tensor_reduce(rq8[s1 % 2][:, :],
                                        qhat[:, q1 * 1024:(q1 + 1) * 1024].rearrange("p (h d) -> p h d", h=8),
                                        AX.X, ALU.add))
            qp["qred"][s1] = dve.n
        if 0 <= s - 1 < NQS and (s - 1) % 2 == 0:
            tq = (s - 1) // 2 + 3
            if tq < NTILE_MY:
                @on("sync")
                def _(s_, tq=tq):
                    dve.wait(s_, qp["xn"].get(tq - 3, 0))
                    sqx[tq % 3].dma(s_.dma_start(out=xa[tq % 3][:, :], in_=x_my[tq * 128:(tq + 1) * 128, :]))

    qt_evac = dve.n
    @on("sync")
    def _(s):
        dve.wait(s, qt_evac)
        sq.dma(s.dma_start(out=qtd[:, :], in_=qtstage[:, :]))
    qtd_stored = sq.n

    a0_pe = pe.n
    a0_act = act.n
    a0_dve = dve.n
    a0_hand = dve.n
    a0_sqx = [sqx[i].n for i in range(3)]
    a0w.close()
    kvY = ExitStack()
    KT_sb = kvY.enter_context(nc.sbuf_tensor("KT_sb", [128, NKV * NT], BF16, side="right"))
    V_sb = kvY.enter_context(nc.sbuf_tensor("V_sb", [128, NTILE_ALL * KV], BF16, side="right"))

    a1P = ExitStack()
    kps2 = [a1P.enter_context(nc.psum_tensor(f"kps2_{i}", [128, KV], F32)) for i in range(2)]
    vps2 = [a1P.enter_context(nc.psum_tensor(f"vps2_{i}", [128, KV], F32)) for i in range(2)]
    ktp_ps = a1P.enter_context(nc.psum_tensor("ktp_ps", [128, KV], BF16))
    ksq_ps = a1P.enter_context(nc.psum_tensor("ksq_ps", [128, KV], F32))

    lo, hi = 0, NTILE_ALL
    sqd, rstdd, var_d, xnd, tped, trd, mmd = {}, {}, {}, {}, {}, {}, {}
    ksqd, kredd, krmsd, khatd, vcopd, ktrd, kted = {}, {}, {}, {}, {}, {}, {}
    vsqd, vredd, vrmsd, vnd_d = {}, {}, {}, {}

    for i in range(lo - 3, hi + 3):
        t = i + 3
        if lo + 3 <= t < hi:
            @on("sync")
            def _(s, t=t):
                dve.wait(s, xnd.get(t - 3, a0_dve))
                sqx[t % 3].dma(s.dma_start(out=xa[t % 3][:, :], in_=x_my[t * 128:(t + 1) * 128, :]))
        elif lo <= t < lo + 3:
            @on("sync")
            def _(s, t=t):
                dve.wait(s, a0_dve)
                sqx[t % 3].dma(s.dma_start(out=xa[t % 3][:, :], in_=x_my[t * 128:(t + 1) * 128, :]))

        t = i
        if lo <= t < hi:
            @on("tensor")
            def _(te, t=t):
                if t == lo:
                    swk.wait(te)
                    swv.wait(te)
                dve.wait(te, xnd[t])
                dve.wait(te, tped.get(t - 1, a0_dve))
                for r in range(2):
                    for j in range(8):
                        c = r * 8 + j
                        pe.inc(te.transpose(tp_ps[r][:, j * 128:(j + 1) * 128],
                                            xn[t % 2][:, c * 128:(c + 1) * 128], eye_sb[:, :]))
            trd[t] = pe.n
        t = i - 3
        if lo <= t < hi:
            @on("tensor")
            def _(te, t=t):
                dve.wait(te, khatd[t])
                dve.wait(te, kted.get(t - 1, a0_dve))
                if t == lo:
                    dve.wait(te, a0_hand)
                for j in range(NKV):
                    pe.inc(te.transpose(ktp_ps[:, j * 128:(j + 1) * 128],
                                        khat[t % 2][:, j * 128:(j + 1) * 128], eye_sb[:, :]))
            ktrd[t] = pe.n
        t = i - 1
        if lo <= t < hi:
            @on("tensor")
            def _(te, t=t):
                dve.wait(te, tped[t])
                act.wait(te, ksqd.get(t - 2, a0_act))
                dve.wait(te, vcopd.get(t - 2, a0_dve))
                if t < lo + 2:
                    dve.wait(te, a0_hand)
                for cb in range(NCB):
                    lhs = xnT[t % 2][:, cb * 128:(cb + 1) * 128]
                    pe.inc(te.matmul(kps2[t % 2][:, :], lhs, wk_sb[:, cb * KV:(cb + 1) * KV],
                                     start=(cb == 0), stop=(cb == NCB - 1)))
                    pe.inc(te.matmul(vps2[t % 2][:, :], lhs, wv_sb[:, cb * KV:(cb + 1) * KV],
                                     start=(cb == 0), stop=(cb == NCB - 1)))
            mmd[t] = pe.n

        t = i - 2
        if lo <= t < hi:
            @on("scalar")
            def _(a, t=t):
                act.wait(a)
                dve.wait(a, kredd[t])
                act.inc(a.activation(rk4[t % 2][:, :], rk4[t % 2][:, :], AF.Ln, scale=1.0 / HD,
                                     bias=consts_sb[:, C_EPS_RMS:C_EPS_RMS + 1]))
                act.wait(a)
                act.inc(a.activation(rk4[t % 2][:, :], rk4[t % 2][:, :], AF.Exp, scale=-0.5))
            krmsd[t] = act.n
        t = i - 2
        if lo <= t < min(NTILE_MY, hi):
            @on("scalar")
            def _(a, t=t):
                act.wait(a)
                dve.wait(a, vredd[t])
                act.inc(a.activation(rv4[t % 2][:, :], rv4[t % 2][:, :], AF.Ln,
                                     bias=consts_sb[:, C_VNEPS:C_VNEPS + 1]))
                act.wait(a)
                act.inc(a.activation(rv4[t % 2][:, :], rv4[t % 2][:, :], AF.Exp, scale=-0.5))
            vrmsd[t] = act.n
        t = i + 2
        if lo <= t < hi:
            @on("scalar")
            def _(a, t=t):
                act.wait(a)
                sqx[t % 3].wait(a, a0_sqx[t % 3] + 16 * (t // 3 + 1))
                pe.wait(a, trd.get(t - 2, a0_pe))
                act.inc(a.activation(xn[t % 2][:, :], xa[t % 3][:, :], AF.Square, accum_out=st8[t % 2][:, 1:2]))
            sqd[t] = act.n
        t = i + 1
        if lo <= t < hi:
            @on("scalar")
            def _(a, t=t):
                act.wait(a)
                dve.wait(a, var_d[t])
                act.inc(a.activation(st8[t % 2][:, 4:5], st8[t % 2][:, 2:3], AF.Ln, scale=1.0 / D,
                                     bias=consts_sb[:, C_EPS_LN:C_EPS_LN + 1]))
                act.wait(a)
                act.inc(a.activation(st8[t % 2][:, 3:4], st8[t % 2][:, 4:5], AF.Exp, scale=-0.5))
            rstdd[t] = act.n
        t = i - 1
        if lo <= t < hi:
            @on("scalar")
            def _(a, t=t):
                act.wait(a)
                pe.wait(a, mmd[t])
                dve.wait(a, kredd.get(t - 1, a0_dve))
                act.inc(a.activation(ksq_ps[:, :], kps2[t % 2][:, :], AF.Square))
                if t < NTILE_MY:
                    svn2[t % 2].wait(a, vnd_d.get(t - 2, 0))
                    act.inc(a.activation(vn1[t % 2][:, :], vps2[t % 2][:, :], AF.Square))
            ksqd[t] = act.n

        t = i + 2
        if lo <= t < hi:
            @on("vector")
            def _(v, t=t):
                dve.wait(v)
                sqx[t % 3].wait(v, a0_sqx[t % 3] + 16 * (t // 3 + 1))
                dve.inc(v.tensor_reduce(st8[t % 2][:, 0:1], xa[t % 3][:, :], AX.X, ALU.add))
                act.wait(v, sqd[t])
                dve.wait(v)
                dve.inc(v.tensor_tensor(st8[t % 2][:, 2:3], st8[t % 2][:, 0:1], st8[t % 2][:, 0:1], ALU.mult))
                dve.wait(v)
                dve.inc(v.scalar_tensor_tensor(st8[t % 2][:, 2:3], st8[t % 2][:, 2:3], -1.0 / D,
                                               st8[t % 2][:, 1:2], ALU.mult, ALU.add))
            var_d[t] = dve.n
        t = i + 1
        if lo <= t < hi:
            @on("vector")
            def _(v, t=t):
                dve.wait(v)
                act.wait(v, rstdd[t])
                pe.wait(v, trd.get(t - 2, a0_pe))
                dve.inc(v.tensor_tensor(st8[t % 2][:, 5:6], st8[t % 2][:, 0:1], st8[t % 2][:, 3:4], ALU.mult))
                dve.wait(v)
                dve.inc(v.tensor_scalar(st8[t % 2][:, 5:6], st8[t % 2][:, 5:6], -1.0 / D, None, ALU.mult))
                dve.wait(v)
                dve.inc(v.tensor_scalar(xn[t % 2][:, :], xa[t % 3][:, :], st8[t % 2][:, 3:4],
                                        st8[t % 2][:, 5:6], ALU.mult, ALU.add))
            xnd[t] = dve.n
        t = i
        if lo <= t < hi:
            @on("vector")
            def _(v, t=t):
                dve.wait(v)
                pe.wait(v, trd[t])
                pe.wait(v, mmd.get(t - 2, a0_pe))
                dve.inc(v.tensor_copy(xnT[t % 2][:, 0:1024], tp_ps[0][:, :]))
                dve.wait(v)
                dve.inc(v.tensor_copy(xnT[t % 2][:, 1024:2048], tp_ps[1][:, :]))
            tped[t] = dve.n
        t = i - 2
        if lo <= t < hi:
            @on("vector")
            def _(v, t=t):
                dve.wait(v)
                act.wait(v, krmsd[t])
                pe.wait(v, ktrd.get(t - 2, a0_pe))
                for j in range(NKV):
                    dve.inc(v.tensor_scalar(khat[t % 2][:, j * 128:(j + 1) * 128],
                                            kps2[t % 2][:, j * 128:(j + 1) * 128],
                                            rk4[t % 2][:, j:j + 1], None, ALU.mult))
                if t < lo + 2:
                    sq.wait(v, qtd_stored)
                    pe.wait(v, a0_pe)
                    act.wait(v, a0_act)
                dve.wait(v)
                dve.inc(v.tensor_copy(V_sb[:, t * KV:(t + 1) * KV], vps2[t % 2][:, :]))
                if t < NTILE_MY:
                    act.wait(v, vrmsd[t])
                    for j in range(NKV):
                        dve.inc(v.tensor_scalar(vn1[t % 2][:, j * 128:(j + 1) * 128],
                                                vps2[t % 2][:, j * 128:(j + 1) * 128],
                                                rv4[t % 2][:, j:j + 1], None, ALU.mult))
            khatd[t] = dve.n
            vcopd[t] = dve.n
            if t < NTILE_MY:
                @on("sync")
                def _(s, t=t):
                    dve.wait(s, vcopd[t])
                    svn2[t % 2].dma(s.dma_start(out=vnd[:, t * KV:(t + 1) * KV], in_=vn1[t % 2][:, :]))
                vnd_d[t] = svn2[t % 2].n
        t = i - 3
        if lo <= t < hi:
            @on("vector")
            def _(v, t=t):
                dve.wait(v)
                pe.wait(v, ktrd[t])
                if t < lo + 2:
                    sq.wait(v, qtd_stored)
                dve.inc(v.tensor_copy(
                    KT_sb[:, :].rearrange("p (h q) -> p h q", h=NKV)[:, :, t * 128:(t + 1) * 128],
                    ktp_ps[:, :].rearrange("p (j q) -> p j q", j=NKV)))
            kted[t] = dve.n
        t = i - 1
        if lo <= t < hi:
            @on("vector")
            def _(v, t=t):
                dve.wait(v)
                act.wait(v, ksqd[t])
                dve.inc(v.tensor_reduce(rk4[t % 2][:, :],
                                        ksq_ps[:, :].rearrange("p (h d) -> p h d", h=4),
                                        AX.X, ALU.add))
                if t < NTILE_MY:
                    dve.wait(v)
                    dve.inc(v.tensor_reduce(rv4[t % 2][:, :],
                                            vn1[t % 2][:, :].rearrange("p (h d) -> p h d", h=4),
                                            AX.X, ALU.add))
            kredd[t] = dve.n
            vredd[t] = dve.n

    if upto == "A":
        dump("d_kt", KT_sb[:, :], [128, NKV * NT], BF16)
        dump("d_v", V_sb[:, :], [128, NTILE_ALL * KV], BF16)
        dump("d_vn", vnd, [128, NTILE_MY * KV], BF16)
        dump("d_qt", qtd, [128, NH * TPC], BF16)
        finish(recs, dumps, [sq, gq, pe, act, dve, svn])
        replay()
        return nc

    a1P.close()
    aC.close()

    Y_sb = kvY.enter_context(nc.sbuf_tensor("Y_sb", [128, NTILE_MY * D], BF16, side="right"))
    bC = ExitStack()
    vn_sb = bC.enter_context(nc.sbuf_tensor("vn_sb", [128, NTILE_MY * KV], BF16))
    padd = [bC.enter_context(nc.sbuf_tensor(f"padd{i}", [128, 512], BF16)) for i in range(2)]
    pacc = [bC.enter_context(nc.sbuf_tensor(f"pacc{i}", [128, 512], F32)) for i in range(4)]
    rdnT_sb = bC.enter_context(nc.sbuf_tensor("rdnT_sb", [128, 4], F32))
    qth = [bC.enter_context(nc.sbuf_tensor(f"qth{i}", [128, TPC], BF16)) for i in range(2)]
    P_sb = [bC.enter_context(nc.sbuf_tensor(f"P_sb{i}", [128, 1024], BF16)) for i in range(4)]
    yTs_sb = bC.enter_context(nc.sbuf_tensor("yTs_sb", [128, 512], BF16))
    cscr = bC.enter_context(nc.sbuf_tensor("cscr", [128, 128], BF16))
    dot_sb = bC.enter_context(nc.sbuf_tensor("dot_sb", [128, 4], F32))

    S_ps = [bC.enter_context(nc.psum_tensor(f"S_ps{i}", [128, 1024], F32)) for i in range(2)]
    yT_ps = [bC.enter_context(nc.psum_tensor(f"yT_ps{i}", [128, 512], F32)) for i in range(2)]
    epi_ps = bC.enter_context(nc.psum_tensor("epi_ps", [128, 8], F32))
    tt_ps = bC.enter_context(nc.psum_tensor("tt_ps", [128, 512], BF16))

    NG = NTILE_ALL // 2

    qth_free = [0, 0]
    qth_loaded = [0, 0]
    dn_mm = 0
    pending_epi = [None, None]
    last_paddd = [{}]
    epi_done = dve.n
    corr_done = dve.n
    b_psum_init = dve.n

    @on("sync")
    def _(s):
        sq.wait(s, qtd_stored)
        svn2[0].wait(s)
        svn2[1].wait(s)
        dve.wait(s)
        pe.wait(s)
        act.wait(s)
        qs[0].dma(s.dma_start(out=qth[0][:, :], in_=qtd[:, 0:TPC]))
        sq.dma(s.dma_start(out=vn_sb[:, :], in_=vnd))
    qth_loaded[0] = qs[0].n
    vn_loaded = sq.n

    for h in range(B_HEADS):
        par = h % 2
        kvh = h // NKV
        if h + 1 < NH:
            @on("sync")
            def _(s):
                if h >= 1:
                    pe.wait(s, qth_free[(h + 1) % 2])
                qs[(h + 1) % 2].dma(s.dma_start(out=qth[(h + 1) % 2][:, :], in_=qtd[:, (h + 1) * TPC:(h + 2) * TPC]))
            qth_loaded[(h + 1) % 2] = qs[(h + 1) % 2].n

        for qb in range(2):
            qbp = (h * 2 + qb) % 2
            rhs_q = qth[par][:, qb * 512:(qb + 1) * 512]
            e_done = [0] * NG
            pv_done = [0] * NG
            s_done = [0] * NG
            prev_paddd = last_paddd[0]
            paddd = {}
            paccd = {}
            dnd = {}
            for gi in range(NG + 2):
                if gi == 8 and pending_epi[0] is not None:
                    pending_epi[0]()
                    pending_epi[0] = None
                if gi == 14 and pending_epi[1] is not None:
                    pending_epi[1]()
                    pending_epi[1] = None
                if 1 <= gi <= NG:
                    g = gi - 1

                    @on("vector")
                    def _(v, g=g):
                        dve.wait(v)
                        act.wait(v, e_done[g])
                        gpc.wait(v, paccd.get(g - 2, 0))
                        dve.inc(v.tensor_tensor(padd[g % 2][:, :], P_sb[g % 4][:, 0:512],
                                                P_sb[g % 4][:, 512:1024], ALU.add))
                    paddd[g] = dve.n

                    @on("gpsimd")
                    def _(gg, g=g):
                        gpc.wait(gg)
                        dve.wait(gg, paddd[g])
                        pa = pacc[qbp * 2 + g % 2]
                        if g < 2:
                            pe.wait(gg, dn_mm)
                            gpc.inc(gg.tensor_copy(pa[:, :], padd[g % 2][:, :]))
                        else:
                            gpc.inc(gg.tensor_tensor(pa[:, :], pa[:, :], padd[g % 2][:, :], ALU.add))
                    paccd[g] = gpc.n
                if gi < NG:
                    g = gi
                    gp = g % 2

                    @on("tensor")
                    def _(te):
                        if qb == 0 and g == 0:
                            qs[par].wait(te, qth_loaded[par])
                        if h == 0 and qb == 0 and g <= 1:
                            dve.wait(te, b_psum_init)
                        if g >= 2:
                            act.wait(te, e_done[g - 2])
                        for j in range(2):
                            kt = 2 * g + j
                            pe.inc(te.matmul(
                                S_ps[gp][:, j * 512:(j + 1) * 512],
                                KT_sb[:, kvh * NT + kt * 128: kvh * NT + (kt + 1) * 128],
                                rhs_q, start=True, stop=True))
                    s_done[g] = pe.n

                    @on("scalar")
                    def _(a):
                        act.wait(a)
                        pe.wait(a, s_done[g])
                        if g >= 4:
                            pe.wait(a, pv_done[g - 4])
                            dve.wait(a, paddd[g - 4])
                        else:
                            dve.wait(a, prev_paddd.get(28 + g, 0))
                        act.inc(a.activation(P_sb[g % 4][:, :], S_ps[g % 2][:, :], AF.Exp))
                    e_done[g] = act.n

                if gi >= 2:
                    g = gi - 2

                    @on("tensor")
                    def _(te):
                        act.wait(te, e_done[g])
                        if g == 0:
                            dve.wait(te, epi_done)
                        for j in range(2):
                            kt = 2 * g + j
                            pe.inc(te.matmul(
                                yT_ps[qbp][:, :],
                                V_sb[:, kt * KV + kvh * 128: kt * KV + (kvh + 1) * 128],
                                P_sb[g % 4][:, j * 512:(j + 1) * 512],
                                start=(g == 0 and j == 0), stop=(g == NG - 1 and j == 1)))
                    pv_done[g] = pe.n

            def make_epi(h=h, qb=qb, par=par, kvh=kvh, qbp=qbp, e_done=e_done, pv_done=pv_done,
                         s_done=s_done, paddd=paddd, paccd=paccd, dnd=dnd):
              nonlocal dn_mm, epi_done, corr_done
              @on("tensor")
              def _(te):
                  gpc.wait(te, paccd[NG - 1])
                  dve.wait(te, epi_done)
                  for j in range(4):
                      pe.inc(te.matmul(epi_ps[:, j:j + 1], pacc[qbp * 2][:, j * 128:(j + 1) * 128],
                                       onecf_sb[:, 0:1], start=True, stop=False))
                      pe.inc(te.matmul(epi_ps[:, j:j + 1], pacc[qbp * 2 + 1][:, j * 128:(j + 1) * 128],
                                       onecf_sb[:, 0:1], start=False, stop=True))
              dn_mm = pe.n

              @on("vector")
              def _(v):
                  dve.wait(v)
                  pe.wait(v, dn_mm)
                  pe.wait(v, pv_done[NG - 1])
                  dve.inc(v.reciprocal(rdnT_sb[:, :], epi_ps[:, 0:4]))
                  dve.wait(v)
                  dve.inc(v.tensor_scalar(rdnT_sb[:, :], rdnT_sb[:, :], -1.0, None, ALU.mult))
                  dve.wait(v)
                  dve.inc(v.tensor_copy(yTs_sb[:, :], yT_ps[qbp][:, :]))
              rdn_done = dve.n
              ydiv_done = dve.n
              epi_done = dve.n

              def make_epi_b(qbp=qbp, h=h, qb=qb, kvh=kvh):
                nonlocal corr_done
                @on("tensor")
                def _(te):
                    dve.wait(te, ydiv_done)
                    dve.wait(te, corr_done)
                    for j in range(4):
                        pe.inc(te.transpose(tt_ps[:, j * 128:(j + 1) * 128],
                                                   yTs_sb[:, j * 128:(j + 1) * 128], eye_sb[:, :]))
                ytr_done = pe.n

                @on("vector")
                def _(v):
                    dve.wait(v)
                    pe.wait(v, ytr_done)
                    if h == 0 and qb == 0:
                        sq.wait(v, vn_loaded)
                    for j in range(4):
                        qt = qb * 4 + j
                        yv = tt_ps[:, j * 128:(j + 1) * 128]
                        vnv = vn_sb[:, qt * KV + kvh * 128: qt * KV + (kvh + 1) * 128]
                        dve.wait(v)
                        dve.inc(v.tensor_tensor(cscr[:, :], yv, vnv, ALU.mult))
                        dve.wait(v)
                        dve.inc(v.tensor_reduce(dot_sb[:, j:j + 1], cscr[:, :], AX.X, ALU.add))
                    for j in range(4):
                        qt = qb * 4 + j
                        yv = tt_ps[:, j * 128:(j + 1) * 128]
                        vnv = vn_sb[:, qt * KV + kvh * 128: qt * KV + (kvh + 1) * 128]
                        dve.wait(v)
                        dve.inc(v.scalar_tensor_tensor(cscr[:, :], vnv, dot_sb[:, j:j + 1], yv,
                                                       ALU.mult, ALU.subtract))
                        dve.wait(v)
                        dve.inc(v.tensor_scalar(
                            Y_sb[:, qt * D + h * 128: qt * D + (h + 1) * 128],
                            cscr[:, :], rdnT_sb[:, j:j + 1], None, ALU.mult))
                corr_done = dve.n
              pending_epi[1] = make_epi_b

            pending_epi[0] = make_epi
            last_paddd[0] = paddd
        qth_free[par] = pe.n

    pending_epi[0]()
    pending_epi[0] = None
    pending_epi[1]()
    pending_epi[1] = None

    if upto == "B":
        dump("d_y", Y_sb[:, :], [128, NTILE_MY * D], BF16)
        dump("d_yts", yTs_sb[:, :], [128, 512], BF16)
        dump("d_p", P_sb[1][:, :], [128, 1024], BF16)
        dump("d_qth", qth[0][:, :], [128, TPC], BF16)
        dump("d_kt", KT_sb[:, :], [128, NKV * NT], BF16)
        finish(recs, dumps, [sq, gq, pe, act, dve])
        replay()
        return nc

    b_end_dve = dve.n
    @on("sync")
    def _(s):
        dve.wait(s, b_end_dve)
        syst.dma(s.dma_start(out=ydram, in_=Y_sb[:, :]))
    b_end_pe = pe.n
    b_end_act = act.n
    if upto == "Cy":
        dump("d_ydram", ydram, [128, NTILE_MY * D], BF16)
        finish(recs, dumps, [sq, gq, pe, act, dve, syst])
        replay()
        return nc
    bC.close()
    kvY.close()

    xn2T = ctx.enter_context(nc.sbuf_tensor("xn2T", [128, NCB * TPC], BF16))
    cw = ExitStack()
    wo_sb = cw.enter_context(nc.sbuf_tensor("wo_sb", [128, NCB * D], BF16))
    ytile = [cw.enter_context(nc.sbuf_tensor(f"ytile{i}", [128, D], BF16)) for i in range(2)]
    YTt = cw.enter_context(nc.sbuf_tensor("YTt", [128, D], BF16))
    xa2 = [cw.enter_context(nc.sbuf_tensor(f"xa2_{i}", [128, D], F32)) for i in range(2)]
    x2_sb = cw.enter_context(nc.sbuf_tensor("x2_sb", [128, D], F32))
    xn2b = cw.enter_context(nc.sbuf_tensor("xn2b", [128, D], BF16))
    sq2 = cw.enter_context(nc.sbuf_tensor("sq2", [128, D], BF16))
    st2 = cw.enter_context(nc.sbuf_tensor("st2", [128, 8], F32))
    wops = cw.enter_context(nc.psum_tensor("wops", [128, D], F32))
    ctp = cw.enter_context(nc.psum_tensor("ctp", [128, 1024], BF16))

    @on("gpsimd")
    def _(g):
        syst.wait(g)
        pe.wait(g, b_end_pe)
        act.wait(g, b_end_act)
        dve.wait(g, b_end_dve)
        for cq in range(4):
            swo.dma(g.dma_start(
                out=wo_sb[:, cq * 4 * D:(cq + 1) * 4 * D].rearrange("p (c o) -> p c o", c=4),
                in_=woT.rearrange("(c p) o -> p c o", p=128)[:, cq * 4:(cq + 1) * 4, :]))

    @on("sync")
    def _(s):
        syst.wait(s)
        pe.wait(s, b_end_pe)
        dve.wait(s, b_end_dve)
        for i in range(2):
            syy[i].dma(s.dma_start(out=ytile[i][:, :], in_=ydram[:, i * D:(i + 1) * D]))
            sqx[i].dma(s.dma_start(out=xa2[i][:, :], in_=x_my[i * 128:(i + 1) * 128, :]))
    c_y0 = [syy[0].n, syy[1].n]
    c_x0 = [sqx[0].n, sqx[1].n]

    cstate = dict(ytile_free=[0, 0], xa2_free=[0, 0], YTt_free=0, ctp_free=0,
                  wops_free=dve.n, x2_free=0, xn2b_free=0)
    for t in range(NTILE_MY):
        par = t % 2
        for r in range(2):
            @on("tensor")
            def _(te):
                syy[par].wait(te, c_y0[par] + 16 * (t // 2))
                if t == 0 and r == 0:
                    dve.wait(te, b_end_dve)
                dve.wait(te, cstate["ctp_free"])
                if r == 0:
                    pe.wait(te, cstate["YTt_free"])
                for j in range(8):
                    c = r * 8 + j
                    pe.inc(te.transpose(ctp[:, j * 128:(j + 1) * 128],
                                               ytile[par][:, c * 128:(c + 1) * 128], eye_sb[:, :]))
            tr_done = pe.n

            @on("vector")
            def _(v):
                dve.wait(v)
                pe.wait(v, tr_done)
                dve.inc(v.tensor_copy(YTt[:, r * 1024:(r + 1) * 1024], ctp[:, :]))
            cstate["ctp_free"] = dve.n
        cstate["ytile_free"][par] = pe.n
        yt_ready = dve.n

        if t + 2 < NTILE_MY:
            @on("sync")
            def _(s):
                pe.wait(s, cstate["ytile_free"][par])
                syy[par].dma(s.dma_start(out=ytile[par][:, :], in_=ydram[:, (t + 2) * D:(t + 3) * D]))

        @on("tensor")
        def _(te):
            dve.wait(te, yt_ready)
            if t == 0:
                swo.wait(te)
            dve.wait(te, cstate["wops_free"])
            for cb in range(NCB):
                lhs = YTt[:, cb * 128:(cb + 1) * 128]
                for oc in range(4):
                    pe.inc(te.matmul(wops[:, oc * 512:(oc + 1) * 512], lhs,
                                            wo_sb[:, cb * D + oc * 512: cb * D + (oc + 1) * 512],
                                            start=(cb == 0), stop=(cb == NCB - 1)))
        wo_done = pe.n
        cstate["YTt_free"] = pe.n

        @on("scalar")
        def _(a):
            act.wait(a)
            pe.wait(a, wo_done)
            act.inc(a.activation(sq2[:, :], wops[:, :], AF.Square, accum_out=st2[:, 1:2]))
        c_ssq = act.n

        @on("vector")
        def _(v):
            dve.wait(v)
            act.wait(v, c_ssq)
            dve.inc(v.tensor_reduce(st2[:, 0:1], wops[:, :], AX.X, ALU.add))
            dve.wait(v)
            dve.inc(v.tensor_tensor(st2[:, 2:3], st2[:, 0:1], st2[:, 0:1], ALU.mult))
            dve.wait(v)
            dve.inc(v.scalar_tensor_tensor(st2[:, 2:3], st2[:, 2:3], -1.0 / D, st2[:, 1:2], ALU.mult, ALU.add))
        c_var = dve.n

        @on("scalar")
        def _(a):
            act.wait(a)
            dve.wait(a, c_var)
            act.inc(a.activation(st2[:, 4:5], st2[:, 2:3], AF.Ln, scale=1.0 / D,
                                 bias=consts_sb[:, C_EPS_LN:C_EPS_LN + 1]))
            act.wait(a)
            act.inc(a.activation(st2[:, 3:4], st2[:, 4:5], AF.Exp, scale=-0.5))
        c_rstd = act.n

        @on("vector")
        def _(v):
            dve.wait(v)
            act.wait(v, c_rstd)
            sqx[par].wait(v, c_x0[par] + 16 * (t // 2))
            dve.inc(v.tensor_tensor(st2[:, 5:6], st2[:, 0:1], st2[:, 3:4], ALU.mult))
            dve.wait(v)
            dve.inc(v.tensor_scalar(st2[:, 5:6], st2[:, 5:6], -1.0 / D, None, ALU.mult))
            dve.wait(v)
            sx2st.wait(v, 16 * t)
            dve.inc(v.tensor_scalar(xa2[par][:, :], xa2[par][:, :], 1.0, st2[:, 5:6], ALU.mult, ALU.add))
            dve.wait(v)
            dve.inc(v.scalar_tensor_tensor(x2_sb[:, :], wops[:, :], st2[:, 3:4], xa2[par][:, :],
                                           ALU.mult, ALU.add))
        cstate["wops_free"] = dve.n
        cstate["xa2_free"][par] = dve.n
        x2_ready = dve.n

        @on("sync")
        def _(s):
            dve.wait(s, x2_ready)
            sx2st.dma(s.dma_start(out=out[t * 128:(t + 1) * 128, :], in_=x2_sb[:, :]))
            if t + 2 < NTILE_MY:
                dve.wait(s, cstate["xa2_free"][par])
                sqx[par].dma(s.dma_start(out=xa2[par][:, :], in_=x_my[(t + 2) * 128:(t + 3) * 128, :]))

        @on("scalar")
        def _(a):
            act.wait(a)
            dve.wait(a, x2_ready)
            act.inc(a.activation(sq2[:, :], x2_sb[:, :], AF.Square, accum_out=st2[:, 1:2]))
        c2_ssq = act.n

        @on("vector")
        def _(v):
            dve.wait(v)
            act.wait(v, c2_ssq)
            dve.inc(v.tensor_reduce(st2[:, 0:1], x2_sb[:, :], AX.X, ALU.add))
            dve.wait(v)
            dve.inc(v.tensor_tensor(st2[:, 2:3], st2[:, 0:1], st2[:, 0:1], ALU.mult))
            dve.wait(v)
            dve.inc(v.scalar_tensor_tensor(st2[:, 2:3], st2[:, 2:3], -1.0 / D, st2[:, 1:2], ALU.mult, ALU.add))
        c2_var = dve.n

        @on("scalar")
        def _(a):
            act.wait(a)
            dve.wait(a, c2_var)
            act.inc(a.activation(st2[:, 4:5], st2[:, 2:3], AF.Ln, scale=1.0 / D,
                                 bias=consts_sb[:, C_EPS_LN:C_EPS_LN + 1]))
            act.wait(a)
            act.inc(a.activation(st2[:, 3:4], st2[:, 4:5], AF.Exp, scale=-0.5))
        c2_rstd = act.n

        @on("vector")
        def _(v):
            dve.wait(v)
            act.wait(v, c2_rstd)
            dve.inc(v.tensor_tensor(st2[:, 5:6], st2[:, 0:1], st2[:, 3:4], ALU.mult))
            dve.wait(v)
            dve.inc(v.tensor_scalar(st2[:, 5:6], st2[:, 5:6], -1.0 / D, None, ALU.mult))
            pe.wait(v, cstate["xn2b_free"])
            dve.wait(v)
            dve.inc(v.tensor_scalar(xn2b[:, :], x2_sb[:, :], st2[:, 3:4], st2[:, 5:6], ALU.mult, ALU.add))
        cstate["x2_free"] = dve.n
        xn2_ready = dve.n

        for r in range(2):
            @on("tensor")
            def _(te):
                dve.wait(te, xn2_ready)
                dve.wait(te, cstate["ctp_free"])
                for j in range(8):
                    c = r * 8 + j
                    pe.inc(te.transpose(ctp[:, j * 128:(j + 1) * 128],
                                               xn2b[:, c * 128:(c + 1) * 128], eye_sb[:, :]))
            tr2_done = pe.n

            @on("vector")
            def _(v):
                dve.wait(v)
                pe.wait(v, tr2_done)
                dve.inc(v.tensor_copy(
                    xn2T[:, :].rearrange("p (c q) -> p c q", c=NCB)[:, r * 8:(r + 1) * 8, t * 128:(t + 1) * 128],
                    ctp[:, :].rearrange("p (j q) -> p j q", j=8)))
            cstate["ctp_free"] = dve.n
        cstate["xn2b_free"] = pe.n

    if upto == "C":
        dump("d_ytt", YTt[:, :], [128, D], BF16)
        dump("d_x2sb", x2_sb[:, :], [128, D], F32)
        dump("d_xn2t", xn2T[:, :], [128, NCB * TPC], BF16)
        finish(recs, dumps, [sq, gq, pe, act, dve])
        replay()
        return nc

    c_end_pe = pe.n
    c_end_dve = dve.n
    c_end_act = act.n
    cw.close()

    hid = ctx.enter_context(nc.sbuf_tensor("hid", [128, NDFF * TPC], BF16))
    dC = ExitStack()
    vbg_sb = dC.enter_context(nc.sbuf_tensor("vbg_sb", [72, DFF], BF16))
    vbu_sb = dC.enter_context(nc.sbuf_tensor("vbu_sb", [72, DFF], BF16))
    udn_sb = dC.enter_context(nc.sbuf_tensor("udn_sb", [128, NDFF * 72], BF16))
    vbd_sb = dC.enter_context(nc.sbuf_tensor("vbd_sb", [72, D], BF16))
    w72_sb = dC.enter_context(nc.sbuf_tensor("w72_sb", [72, TPC], BF16))
    tgs_sb = dC.enter_context(nc.sbuf_tensor("tgs_sb", [72, TPC], BF16))
    tus_sb = dC.enter_context(nc.sbuf_tensor("tus_sb", [72, TPC], BF16))
    t2s_sb = dC.enter_context(nc.sbuf_tensor("t2s_sb", [72, TPC], BF16))
    dPre = ExitStack()
    rupa_sb = dPre.enter_context(nc.sbuf_tensor("rupa_sb", [128, NCB * 72], BF16))
    rupr_sb = dPre.enter_context(nc.sbuf_tensor("rupr_sb", [128, NCB * 8], BF16))
    rupb_sb = dPre.enter_context(nc.sbuf_tensor("rupb_sb", [128, NCB * 72], BF16))
    msel_sb = dPre.enter_context(nc.sbuf_tensor("msel_sb", [9, 72], BF16))
    ebias_sb = dPre.enter_context(nc.sbuf_tensor("ebias_sb", [1, 8], F32))
    ebias_bc = dPre.enter_context(nc.sbuf_tensor("ebias_bc", [128, 8], F32))
    logit_sb = dPre.enter_context(nc.sbuf_tensor("logit_sb", [128, 8], F32))
    rt8_sb = dPre.enter_context(nc.sbuf_tensor("rt8_sb", [8, TPC], F32))
    w9_sb = dPre.enter_context(nc.sbuf_tensor("w9_sb", [128, 16], F32))
    rst = dPre.enter_context(nc.sbuf_tensor("rst", [128, 8], F32))
    w9b_sb = dPre.enter_context(nc.sbuf_tensor("w9b_sb", [128, 9], BF16))
    wext_sb = dPre.enter_context(nc.sbuf_tensor("wext_sb", [9, TPC], BF16))

    @on("gpsimd")
    def _(g):
        pe.wait(g, c_end_pe)
        dve.wait(g, c_end_dve)
        act.wait(g, c_end_act)
        sx2st.wait(g)
        srup.dma(g.dma_start(out=rupa_sb[:, :].rearrange("p (c o) -> p c o", c=NCB),
                             in_=rupa.rearrange("(c p) o -> p c o", p=128)))
        srup.dma(g.dma_start(out=rupr_sb[:, :].rearrange("p (c o) -> p c o", c=NCB),
                             in_=rupr.rearrange("(c p) o -> p c o", p=128)))
        srup.dma(g.dma_start(out=rupb_sb[:, :].rearrange("p (c o) -> p c o", c=NCB),
                             in_=rupb.rearrange("(c p) o -> p c o", p=128)))
        srup.dma(g.dma_start(out=vbg_sb[:, :], in_=vbg))
        srup.dma(g.dma_start(out=vbu_sb[:, :], in_=vbu))
        srup.dma(g.dma_start(out=udn_sb[:, :].rearrange("p (n r) -> p n r", n=NDFF),
                             in_=udn.rearrange("(n p) r -> p n r", p=128)))
        srup.dma(g.dma_start(out=vbd_sb[:, :], in_=vbd))
        srup.dma(g.dma_start(out=msel_sb[:, :], in_=msel))
        srup.dma(g.dma_start(out=ebias_sb[:, :], in_=ebias))
    n_rup = srup.n

    dP0 = ExitStack()
    rupr_ps = dP0.enter_context(nc.psum_tensor("rupr_ps", [8, TPC], F32))
    eb_ps = dP0.enter_context(nc.psum_tensor("eb_ps", [128, 8], F32))

    @on("tensor")
    def _(te):
        srup.wait(te, n_rup)
        dve.wait(te, c_end_dve)
        pe.inc(te.matmul(eb_ps[:, :], onerf_sb[:, :], ebias_sb[:, :], start=True, stop=True))
        for nh in range(2):
            for cb in range(NCB):
                pe.inc(te.matmul(rupr_ps[:, nh * 512:(nh + 1) * 512],
                                        rupr_sb[:, cb * 8:(cb + 1) * 8],
                                        xn2T[:, cb * TPC + nh * 512: cb * TPC + (nh + 1) * 512],
                                        start=(cb == 0), stop=(cb == NCB - 1)))
    r_mm = pe.n

    @on("vector")
    def _(v):
        dve.wait(v)
        pe.wait(v, r_mm)
        dve.inc(v.tensor_copy(ebias_bc[:, :], eb_ps[:, :]))
        dve.inc(v.tensor_copy(rt8_sb[:, :], rupr_ps[:, :]))
    rt8_done = dve.n
    dP0.close()

    dP = ExitStack()
    tra_ps = dP.enter_context(nc.psum_tensor("tra_ps", [72, TPC], F32))
    trb_ps = dP.enter_context(nc.psum_tensor("trb_ps", [72, TPC], F32))
    w72_ps = dP.enter_context(nc.psum_tensor("w72_ps", [72, TPC], F32))
    rt_ps = dP.enter_context(nc.psum_tensor("rt_ps", [128, 8], F32))
    wext_ps = dP.enter_context(nc.psum_tensor("wext_ps", [9, TPC], BF16))

    @on("tensor")
    def _(te):
        dve.wait(te, rt8_done)
        for nh in range(2):
            for cb in range(NCB):
                pe.inc(te.matmul(tra_ps[:, nh * 512:(nh + 1) * 512],
                                        rupa_sb[:, cb * 72:(cb + 1) * 72],
                                        xn2T[:, cb * TPC + nh * 512: cb * TPC + (nh + 1) * 512],
                                        start=(cb == 0), stop=(cb == NCB - 1)))
            for cb in range(NCB):
                pe.inc(te.matmul(trb_ps[:, nh * 512:(nh + 1) * 512],
                                        rupb_sb[:, cb * 72:(cb + 1) * 72],
                                        xn2T[:, cb * TPC + nh * 512: cb * TPC + (nh + 1) * 512],
                                        start=(cb == 0), stop=(cb == NCB - 1)))
    dpre_mm = pe.n

    for t in range(NTILE_MY):
        @on("tensor")
        def _(te):
            dve.wait(te, rt8_done)
            if t > 0:
                dve.wait(te, rt_evac)
            pe.inc(te.transpose(rt_ps[:, :], rt8_sb[:, t * 128:(t + 1) * 128], eyef8_sb[:, :]))
        rt_tr = pe.n

        @on("vector")
        def _(v):
            dve.wait(v)
            pe.wait(v, rt_tr)
            dve.inc(v.tensor_copy(logit_sb[:, :], rt_ps[:, :]))
        rt_evac = dve.n

        @on("scalar")
        def _(a):
            act.wait(a)
            dve.wait(a, rt_evac)
            act.inc(a.activation(w9_sb[:, 0:8], logit_sb[:, :], AF.Sigmoid))
        sig_done = act.n

        @on("vector")
        def _(v):
            dve.wait(v)
            act.wait(v, sig_done)
            dve.inc(v.tensor_tensor(w9_sb[:, 8:16], w9_sb[:, 0:8], ebias_bc[:, :], ALU.add))
            dve.wait(v)
            dve.inc(v.tensor_reduce(rst[:, 0:1], w9_sb[:, 8:16], AX.X, ALU.max))
            dve.wait(v)
            dve.inc(v.tensor_scalar(logit_sb[:, :], w9_sb[:, 8:16], rst[:, 0:1], None, ALU.is_equal))
            dve.wait(v)
            dve.inc(v.scalar_tensor_tensor(logit_sb[:, :], logit_sb[:, :], -1e30, w9_sb[:, 8:16],
                                           ALU.mult, ALU.add))
            dve.wait(v)
            dve.inc(v.tensor_reduce(rst[:, 1:2], logit_sb[:, :], AX.X, ALU.max))
            dve.wait(v)
            dve.inc(v.tensor_scalar(logit_sb[:, :], w9_sb[:, 8:16], rst[:, 1:2], None, ALU.is_ge))
            dve.wait(v)
            dve.inc(v.tensor_tensor(logit_sb[:, :], logit_sb[:, :], w9_sb[:, 0:8], ALU.mult))
            dve.wait(v)
            dve.inc(v.tensor_reduce(rst[:, 2:3], logit_sb[:, :], AX.X, ALU.add))
            dve.wait(v)
            dve.inc(v.tensor_scalar(rst[:, 2:3], rst[:, 2:3], 1e-8, None, ALU.add))
            dve.wait(v)
            dve.inc(v.reciprocal(rst[:, 3:4], rst[:, 2:3]))
            dve.wait(v)
            dve.inc(v.tensor_scalar(w9b_sb[:, 0:8], logit_sb[:, :], rst[:, 3:4], None, ALU.mult))
            dve.wait(v)
            dve.inc(v.memset(w9b_sb[:, 8:9], 1.0))
        w9_done = dve.n

        @on("tensor")
        def _(te):
            dve.wait(te, w9_done)
            pe.inc(te.transpose(wext_ps[0:9, t * 128:(t + 1) * 128],
                                       w9b_sb[:, 0:9], eye_sb[:, :]))
        w9_tr = pe.n

        @on("vector")
        def _(v):
            dve.wait(v)
            pe.wait(v, w9_tr)
            dve.inc(v.tensor_copy(wext_sb[:, t * 128:(t + 1) * 128], wext_ps[0:9, t * 128:(t + 1) * 128]))
        rt_evac = dve.n
    wext_done = dve.n

    @on("tensor")
    def _(te):
        dve.wait(te, wext_done)
        for nh in range(2):
            pe.inc(te.matmul(w72_ps[:, nh * 512:(nh + 1) * 512], msel_sb[:, :],
                                    wext_sb[:, nh * 512:(nh + 1) * 512], start=True, stop=True))
    w72_mm = pe.n

    @on("vector")
    def _(v):
        dve.wait(v)
        pe.wait(v, w72_mm)
        pe.wait(v, dpre_mm)
        dve.inc(v.tensor_copy(w72_sb[:, :], w72_ps[:, :]))
        dve.wait(v)
        dve.inc(v.tensor_tensor(tgs_sb[:, :], tra_ps[:, :], w72_sb[:, :], ALU.mult))
        dve.wait(v)
        dve.inc(v.tensor_tensor(tus_sb[:, :], trb_ps[:, :], w72_sb[:, :], ALU.mult))
    dpre_done = dve.n

    dP.close()
    dPre.close()

    d1S = ExitStack()
    wg_sb = [d1S.enter_context(nc.sbuf_tensor(f"wg_sb{i}", [128, D], BF16)) for i in range(2)]
    wu_sb = [d1S.enter_context(nc.sbuf_tensor(f"wu_sb{i}", [128, D], BF16)) for i in range(2)]
    sg_sb = d1S.enter_context(nc.sbuf_tensor("sg_sb", [128, 512], BF16))
    d1P = ExitStack()
    gps = [d1P.enter_context(nc.psum_tensor(f"gps{i}", [128, 512], F32)) for i in range(2)]
    ups = [d1P.enter_context(nc.psum_tensor(f"ups{i}", [128, 512], F32)) for i in range(2)]
    t2_ps = d1P.enter_context(nc.psum_tensor("t2_ps", [72, TPC], F32))

    @on("gpsimd")
    def _(g):
        dve.wait(g, dpre_done)
        for i in range(2):
            swg[i].dma(g.dma_start(out=wg_sb[i][:, :].rearrange("p (c o) -> p c o", c=NCB),
                                   in_=wg_r[i].rearrange("(c p) o -> p c o", p=128)))
            swg[i].dma(g.dma_start(out=wu_sb[i][:, :].rearrange("p (c o) -> p c o", c=NCB),
                                   in_=wu_r[i].rearrange("(c p) o -> p c o", p=128)))
    d1_w0 = [swg[0].n, swg[1].n]

    dstate = dict(g_free=[act.n, act.n], u_free=[dve.n, dve.n],
                  sg_free=dve.n, w_free=[0, 0])
    pend_t2 = [None]
    for dffb in range(NDFF):
        wpar = dffb % 2
        for nh in range(2):
            par = nh
            if pend_t2[0] is not None:
                pend_t2[0]()
                pend_t2[0] = None

            @on("tensor")
            def _(te):
                if dffb == 0 and nh == 0:
                    dve.wait(te, dpre_done)
                swg[wpar].wait(te, d1_w0[wpar] + 32 * (dffb // 2))
                act.wait(te, dstate["g_free"][par])
                dve.wait(te, dstate["u_free"][par])
                for cb in range(NCB):
                    pe.inc(te.matmul(gps[par][:, :], wg_sb[wpar][:, cb * 128:(cb + 1) * 128],
                                            xn2T[:, cb * TPC + nh * 512: cb * TPC + (nh + 1) * 512],
                                            start=(cb == 0), stop=False))
                pe.inc(te.matmul(gps[par][:, :], vbg_sb[:, dffb * 128:(dffb + 1) * 128],
                                        tgs_sb[:, nh * 512:(nh + 1) * 512], start=False, stop=True))
                for cb in range(NCB):
                    pe.inc(te.matmul(ups[par][:, :], wu_sb[wpar][:, cb * 128:(cb + 1) * 128],
                                            xn2T[:, cb * TPC + nh * 512: cb * TPC + (nh + 1) * 512],
                                            start=(cb == 0), stop=False))
                pe.inc(te.matmul(ups[par][:, :], vbu_sb[:, dffb * 128:(dffb + 1) * 128],
                                        tus_sb[:, nh * 512:(nh + 1) * 512], start=False, stop=True))
            gu_done = pe.n

            @on("scalar")
            def _(a):
                act.wait(a)
                pe.wait(a, gu_done)
                dve.wait(a, dstate["sg_free"])
                act.inc(a.activation(sg_sb[:, :], gps[par][:, :], AF.Silu))
            silu_done = act.n
            dstate["g_free"][par] = act.n

            @on("vector")
            def _(v):
                dve.wait(v)
                act.wait(v, silu_done)
                dve.inc(v.tensor_tensor(hid[:, dffb * TPC + nh * 512: dffb * TPC + (nh + 1) * 512],
                                        sg_sb[:, :], ups[par][:, :], ALU.mult))
            dstate["u_free"][par] = dve.n
            dstate["sg_free"] = dve.n
            hid_done = dve.n

            def t2_emit(dffb=dffb, nh=nh, hid_done=hid_done):
                @on("tensor")
                def _(te):
                    dve.wait(te, hid_done)
                    pe.inc(te.matmul(t2_ps[:, nh * 512:(nh + 1) * 512],
                                            udn_sb[:, dffb * 72:(dffb + 1) * 72],
                                            hid[:, dffb * TPC + nh * 512: dffb * TPC + (nh + 1) * 512],
                                            start=(dffb == 0), stop=(dffb == NDFF - 1)))
            pend_t2[0] = t2_emit

        if dffb + 2 < NDFF:
            @on("gpsimd")
            def _(g):
                pe.wait(g, pe.n)
                swg[wpar].dma(g.dma_start(out=wg_sb[wpar][:, :].rearrange("p (c o) -> p c o", c=NCB),
                                          in_=wg_r[dffb + 2].rearrange("(c p) o -> p c o", p=128)))
                swg[wpar].dma(g.dma_start(out=wu_sb[wpar][:, :].rearrange("p (c o) -> p c o", c=NCB),
                                          in_=wu_r[dffb + 2].rearrange("(c p) o -> p c o", p=128)))
    pend_t2[0]()
    pend_t2[0] = None
    d1_mm = pe.n

    @on("vector")
    def _(v):
        dve.wait(v)
        pe.wait(v, d1_mm)
        dve.inc(v.tensor_tensor(t2s_sb[:, :], t2_ps[:, :], w72_sb[:, :], ALU.mult))
    t2s_done = dve.n

    if upto == "D1":
        dump("d_hid", hid[:, 0:4 * TPC], [128, 4 * TPC], BF16)
        dump("d_t2s", t2s_sb[:, :], [72, TPC], BF16)
        dump("d_w72", w72_sb[:, :], [72, TPC], BF16)
        finish(recs, dumps, [sq, gq, pe, act, dve])
        replay()
        return nc

    d1P.close()
    d1S.close()

    d2P = ExitStack()
    ops = [d2P.enter_context(nc.psum_tensor(f"ops{i}", [128, 512], F32)) for i in range(NTILE_MY)]
    wdq_sb = [dC.enter_context(nc.sbuf_tensor(f"wdq_sb{i}", [128, 4 * 512], BF16)) for i in range(2)]
    ffn_sb = xn2T
    x2r_sb = dC.enter_context(nc.sbuf_tensor("x2r_sb", [128, D], F32))
    out_sb = dC.enter_context(nc.sbuf_tensor("out_sb", [128, D], F32))

    NQ = 4

    @on("gpsimd")
    def _(g):
        dve.wait(g, t2s_done)
        for i in range(2):
            swd[i].dma(g.dma_start(
                out=wdq_sb[i][:, :].rearrange("p (c o) -> p c o", c=4),
                in_=wd[i * 512:(i + 1) * 512, 0:512].rearrange("(c p) o -> p c o", p=128)))
    wd_loaded = {0: swd[0].n, 1: swd[1].n}

    evac_done = [dve.n] * NTILE_MY
    dn_count = 0
    for j in range(NQ):
        for ch in range(11):
            dffb0 = ch * 4
            nd = min(4, NDFF - dffb0)
            cpar = (j * 11 + ch) % 2

            @on("tensor")
            def _(te):
                swd[cpar].wait(te, wd_loaded[j * 11 + ch])
                if j == 0 and ch == 0:
                    dve.wait(te, t2s_done)
                for dd in range(nd):
                    dffb = dffb0 + dd
                    for qt in range(NTILE_MY):
                        if dffb == 0:
                            dve.wait(te, evac_done[qt])
                        pe.inc(te.matmul(
                            ops[qt][:, :],
                            hid[:, dffb * TPC + qt * 128: dffb * TPC + (qt + 1) * 128],
                            wdq_sb[cpar][:, dd * 512:(dd + 1) * 512],
                            start=(dffb == 0), stop=False))
            dstate["w_free"][cpar] = pe.n

            nxt = j * 11 + ch + 2
            if nxt < NQ * 11:
                nj, nch = divmod(nxt, 11)
                ndffb0 = nch * 4
                nnd = min(4, NDFF - ndffb0)
                @on("gpsimd")
                def _(g):
                    pe.wait(g, dstate["w_free"][nxt % 2])
                    swd[nxt % 2].dma(g.dma_start(
                        out=wdq_sb[nxt % 2][:, 0:nnd * 512].rearrange("p (c o) -> p c o", c=nnd),
                        in_=wd[ndffb0 * 128:(ndffb0 + nnd) * 128, nj * 512:(nj + 1) * 512].rearrange("(c p) o -> p c o", p=128)))
                wd_loaded[nxt] = swd[nxt % 2].n

        for qt in range(NTILE_MY):
            @on("tensor")
            def _(te):
                pe.inc(te.matmul(ops[qt][:, :], t2s_sb[:, qt * 128:(qt + 1) * 128],
                                        vbd_sb[:, j * 512:(j + 1) * 512], start=False, stop=True))
            q_done = pe.n

            @on("vector")
            def _(v):
                dve.wait(v)
                pe.wait(v, q_done)
                dve.inc(v.tensor_copy(ffn_sb[:, qt * D + j * 512: qt * D + (j + 1) * 512], ops[qt][:, :]))
            evac_done[qt] = dve.n

    d2_end = dve.n
    x2r_free = dve.n
    out_free = dve.n
    sq2d = dC.enter_context(nc.sbuf_tensor("sq2d", [128, D], BF16))
    std_ = dC.enter_context(nc.sbuf_tensor("std_", [128, 8], F32))
    for t in range(NTILE_MY):
        @on("sync")
        def _(s):
            sx2st.wait(s)
            dve.wait(s, x2r_free)
            sx2ld.dma(s.dma_start(out=x2r_sb[:, :], in_=out[t * 128:(t + 1) * 128, :]))
        x2_ld = sx2ld.n

        @on("scalar")
        def _(a):
            act.wait(a)
            dve.wait(a, evac_done[t])
            act.inc(a.activation(sq2d[:, :], ffn_sb[:, t * D:(t + 1) * D], AF.Square, accum_out=std_[:, 1:2]))
        d_ssq = act.n

        @on("vector")
        def _(v):
            dve.wait(v)
            act.wait(v, d_ssq)
            dve.inc(v.tensor_reduce(std_[:, 0:1], ffn_sb[:, t * D:(t + 1) * D], AX.X, ALU.add))
            dve.wait(v)
            dve.inc(v.tensor_tensor(std_[:, 2:3], std_[:, 0:1], std_[:, 0:1], ALU.mult))
            dve.wait(v)
            dve.inc(v.scalar_tensor_tensor(std_[:, 2:3], std_[:, 2:3], -1.0 / D, std_[:, 1:2], ALU.mult, ALU.add))
        d_var = dve.n

        @on("scalar")
        def _(a):
            act.wait(a)
            dve.wait(a, d_var)
            act.inc(a.activation(std_[:, 4:5], std_[:, 2:3], AF.Ln, scale=1.0 / D,
                                 bias=consts_sb[:, C_EPS_LN:C_EPS_LN + 1]))
            act.wait(a)
            act.inc(a.activation(std_[:, 3:4], std_[:, 4:5], AF.Exp, scale=-0.5))
        d_rstd = act.n

        @on("vector")
        def _(v):
            dve.wait(v)
            act.wait(v, d_rstd)
            sx2ld.wait(v, x2_ld)
            dve.inc(v.tensor_tensor(std_[:, 5:6], std_[:, 0:1], std_[:, 3:4], ALU.mult))
            dve.wait(v)
            dve.inc(v.tensor_scalar(std_[:, 5:6], std_[:, 5:6], -1.0 / D, None, ALU.mult))
            dve.wait(v)
            dve.inc(v.tensor_scalar(x2r_sb[:, :], x2r_sb[:, :], 1.0, std_[:, 5:6], ALU.mult, ALU.add))
            dve.wait(v)
            sout.wait(v, 16 * t)
            dve.inc(v.scalar_tensor_tensor(out_sb[:, :], ffn_sb[:, t * D:(t + 1) * D], std_[:, 3:4],
                                           x2r_sb[:, :], ALU.mult, ALU.add))
        x2r_free = dve.n
        out_ready = dve.n

        @on("sync")
        def _(s):
            dve.wait(s, out_ready)
            sout.dma(s.dma_start(out=out[t * 128:(t + 1) * 128, :], in_=out_sb[:, :]))
        dn_count += 1

    finish(recs, dumps, [sq, gq, pe, act, dve, sout, sx2st, syst])
    replay()
    return nc


def finish(recs, dumps, sems):
    sq = sems[0]
    s = recs["gpsimd"]
    for sem in sems[1:]:
        sem.wait(s)
    for d, ap in dumps:
        sq.dma(s.dma_start(out=d, in_=ap))
    sq.wait(s)


def prep_inputs(I):
    f32 = np.float32
    x = np.ascontiguousarray(np.asarray(I["L0_attn_x"], dtype=f32).reshape(NT, D))
    wqT = np.ascontiguousarray(np.asarray(I["L0_attn_Wq"], f32).T)
    wkT = np.ascontiguousarray(np.asarray(I["L0_attn_Wk"], f32).T)
    wvT = np.ascontiguousarray(np.asarray(I["L0_attn_Wv"], f32).T)
    woT = np.ascontiguousarray(np.asarray(I["L0_attn_Wo"], f32).T)
    wg = np.asarray(I["L0_ffn_W_gate"], f32)
    wu = np.asarray(I["L0_ffn_W_up"], f32)
    wd = np.ascontiguousarray(np.asarray(I["L0_ffn_W_down"], f32))
    wg_r = np.ascontiguousarray(wg.reshape(NCB, 128, NDFF, 128).transpose(2, 1, 0, 3).reshape(NDFF, 128, D))
    wu_r = np.ascontiguousarray(wu.reshape(NCB, 128, NDFF, 128).transpose(2, 1, 0, 3).reshape(NDFF, 128, D))
    ug = np.concatenate([np.asarray(I["L0_ffn_sA_gate"], f32),
                         np.asarray(I["L0_ffn_A_gate"], f32).transpose(1, 0, 2).reshape(D, 64)], axis=1)
    uu = np.concatenate([np.asarray(I["L0_ffn_sA_up"], f32),
                         np.asarray(I["L0_ffn_A_up"], f32).transpose(1, 0, 2).reshape(D, 64)], axis=1)
    rupa = np.ascontiguousarray(ug)
    rupr = np.ascontiguousarray(np.asarray(I["L0_ffn_W_router"], f32))
    rupb = np.ascontiguousarray(uu)
    vbg = np.concatenate([np.asarray(I["L0_ffn_sB_gate"], f32),
                          np.asarray(I["L0_ffn_B_gate"], f32).reshape(64, DFF)], axis=0)
    vbu = np.concatenate([np.asarray(I["L0_ffn_sB_up"], f32),
                          np.asarray(I["L0_ffn_B_up"], f32).reshape(64, DFF)], axis=0)
    udn = np.concatenate([np.asarray(I["L0_ffn_sA_down"], f32),
                          np.asarray(I["L0_ffn_A_down"], f32).transpose(1, 0, 2).reshape(DFF, 64)], axis=1)
    vbd = np.concatenate([np.asarray(I["L0_ffn_sB_down"], f32),
                          np.asarray(I["L0_ffn_B_down"], f32).reshape(64, D)], axis=0)
    msel = np.zeros((9, 72), BF)
    msel[8, 0:8] = BF(0.125)
    for e in range(8):
        msel[e, 8 + 8 * e: 16 + 8 * e] = BF(0.125)
    ebias = np.ascontiguousarray(np.asarray(I["L0_ffn_expert_bias"], f32).reshape(1, 8))
    consts = np.zeros((128, 8), f32)
    consts[:, C_EPS_LN] = 1e-5
    consts[:, C_EPS_RMS] = RMS_EPS
    consts[:, C_QBIAS] = -0.5 * np.log(128.0)
    consts[:, C_VNEPS] = 1e-24
    common = dict(
        wqT=wqT, wkT=wkT, wvT=wvT, woT=woT, wg_r=wg_r, wu_r=wu_r, wd=wd,
        rupa=rupa, rupr=rupr, rupb=rupb, vbg=vbg, vbu=vbu, udn=udn, vbd=vbd, msel=msel,
        ebias=ebias, eye=np.eye(128, dtype=BF), eyef=np.eye(128, dtype=np.float32),
        eyef8=np.eye(8, dtype=np.float32),
        onec=np.ones((128, 1), BF), onecf=np.ones((128, 1), np.float32),
        onerf=np.ones((1, 128), np.float32), consts=consts,
    )
    in_maps = []
    for c in range(NCORE):
        m = dict(common)
        m["x_my"] = np.ascontiguousarray(np.concatenate([x[c * TPC:, :], x[:c * TPC, :]], axis=0))
        in_maps.append(m)
    return in_maps


def kernel(**inputs):
    in_maps = prep_inputs(inputs)
    nc = build("full")
    res = run_bass_kernel_spmd(nc, in_maps, core_ids=list(range(NCORE)))
    out = np.concatenate([np.asarray(r["out"]) for r in res.results], axis=0)
    return out.reshape(1, NT, D).astype(np.float32)
